# version 05
# baseline (speedup 1.0000x reference)
import numpy as np
import ml_dtypes
import concourse.bass as bass
import concourse.mybir as mybir
from concourse.bass_utils import run_bass_kernel_spmd

F32 = mybir.dt.float32
BF16 = mybir.dt.bfloat16
AF = mybir.ActivationFunctionType
ALU = mybir.AluOpType
AX = mybir.AxisListType

T, HID, INTER = 4096, 2048, 5632
NHC, NIC = 16, 44
KB = 1024

SB = {}
SB_BASE = 13 * KB
def _sb(name, off_kb, shape, dtype):
    assert name not in SB
    off = int(off_kb * KB) + SB_BASE
    off = (off + 31) // 32 * 32
    SB[name] = (off, shape, dtype)

_sb("c_m8",     4.000, [128, 1], F32)
_sb("c_eps",    4.0625, [128, 1], F32)
_sb("ones64",   4.125, [1, 64], F32)
_sb("ones128",  4.375, [1, 128], F32)
_sb("gateS",    4.875, [128, 44], F32)
_sb("upS",      5.0625, [128, 44], F32)
_sb("downS",    5.25, [128, 16], F32)
_sb("permM",    5.3125, [128, 128], BF16)
_sb("identM",   5.5625, [128, 128], F32)
_sb("rstd_own", 6.25, [1, 1024], F32)
_sb("nmr_own", 10.25, [1, 1024], F32)
_sb("masks",    14.25, [128, 4 * 512], BF16)
_sb("gu_g",   110.0, [128, 2 * 16 * 128], BF16)
_sb("gu_u",   118.0, [128, 2 * 16 * 128], BF16)
_sb("K_sb",    22.0, [128, 4 * 4096], BF16)
_sb("V_sb",    54.0, [128, 32 * 520], BF16)
_sb("H_sb",    22.0, [128, 44 * 1024], BF16)
_sb("Emask",   88.0, [128, 2048], BF16)
_sb("gtC",    132.5, [128, 1024], F32)
_sb("Ebuf",    92.5, [128, 2 * 2048], BF16)
_sb("rdbuf",  101.0, [1, 512], F32)
_sb("bcbuf0", 103.5, [64, 512], F32)
_sb("bcbuf1", 105.5, [64, 512], F32)
_sb("antmp",  107.5, [64, 2 * 512], BF16)
_sb("xn_rot", 110.0, [128, 16 * 512], BF16)
_sb("xn_rot2", 92.5, [128, 16 * 512], BF16)
_sb("wkv",    126.0, [128, 16 * 1024], BF16)
_sb("Q_sb",   110.0, [128, 16 * 1024], BF16)
_sb("dn_st",  110.0, [128, 2 * 22 * 256], BF16)
_sb("csn",    158.0, [128, 2 * 512], F32)
_sb("xfm_t",  162.0, [128, 2 * 512], F32)
_sb("kbfA",   166.0, [128, 2 * 512], BF16)
_sb("t1A",    168.0, [128, 2 * 512], F32)
_sb("appt",   172.0, [128, 2 * 512], F32)
_sb("wq_st",  142.5, [128, 2 * 16 * 256], BF16)
_sb("cosownQ", 88.0, [128, 1024], F32)
_sb("sinownQ", 92.5, [128, 1024], F32)
_sb("kbfQ",    96.5, [128, 1024], BF16)
_sb("t1Q",    158.6, [128, 1024], F32)
_sb("xn_own", 142.5, [128, 16 * 1024], BF16)
_sb("wo_st",  142.5, [128, 2 * 16 * 256], BF16)
_sb("g_D",    158.5, [128, 1024], F32)
_sb("sg_D",   162.5, [128, 2 * 1024], F32)
_sb("xo_D",   170.5, [128, 2 * 256], F32)
_sb("out_D",  172.5, [128, 4 * 256], F32)
_sb("x_t",    176.5, [128, 2 * 2048], F32)
_sb("sq_j",   192.5, [128, 2048], BF16)
_sb("parts",  196.6, [128, 16], F32)
_sb("scol",   196.7, [128, 16], F32)
_sb("rows_g", 196.8, [1, 512], F32)
_sb("rows_n", 198.8, [1, 512], F32)
_sb("reps",   200.8, [128, 2 * 512], F32)
_sb("xnb_Q",  176.5, [128, 3 * 1024], BF16)
_sb("xfm_ow", 182.5, [128, 2 * 1024], F32)
_sb("reps_o", 190.5, [128, 2 * 1024], F32)
_sb("A_sb",   174.5, [128, 16 * 1024], BF16)
SB["out_D"] = (int(170.5 * KB) + SB_BASE, [128, 4 * 256], F32)
SB["xo_D"] = (int(136.5 * KB) + SB_BASE, [128, 2 * 256], F32)
SB_LIMIT = 229 * KB


def _esz(dt):
    return 2 if dt == BF16 else 4


def _check_map():
    for name, (off, shape, dt) in SB.items():
        n = 1
        for s in shape[1:]:
            n *= s
        assert off + n * _esz(dt) <= SB_LIMIT, f"{name} out of sbuf"
    LIFE = {
        "c_m8": "WAQBCD", "c_eps": "WAQBCD", "ones64": "WAQBCD", "ones128": "WAQBCD",
        "gateS": "WAQBCD", "upS": "WAQBCD", "downS": "WAQBCD", "permM": "WAQBCD",
        "identM": "WAQBCD",
        "rstd_own": "AQ", "nmr_own": "AQ", "masks": "B", "gu_g": "C", "gu_u": "C",
        "K_sb": "AQB", "V_sb": "AQB", "H_sb": "CD",
        "Emask": "B", "gtC": "C", "Ebuf": "B", "rdbuf": "B", "bcbuf0": "B", "bcbuf1": "B", "antmp": "B",
        "xn_rot": "A", "xn_rot2": "A", "wkv": "A", "Q_sb": "QB", "dn_st": "D",
        "csn": "A", "xfm_t": "A", "kbfA": "A", "t1A": "A", "appt": "A",
        "wq_st": "Q", "cosownQ": "Q", "sinownQ": "Q", "kbfQ": "Q", "t1Q": "Q",
        "xn_own": "C", "wo_st": "D", "g_D": "D", "sg_D": "D", "xo_D": "D", "out_D": "D",
        "x_t": "AQ", "sq_j": "AQ", "parts": "AQ", "scol": "AQ",
        "rows_g": "A", "rows_n": "A", "reps": "A",
        "xnb_Q": "Q", "xfm_ow": "Q", "reps_o": "Q", "A_sb": "BCD",
    }
    LIFE["masks"] = "QB"
    names = list(SB)
    for i, a in enumerate(names):
        for b in names[i + 1:]:
            (oa, sa, da), (ob_, sb_, db) = SB[a], SB[b]
            na = _esz(da)
            for s in sa[1:]:
                na *= s
            nb = _esz(db)
            for s in sb_[1:]:
                nb *= s
            if oa < ob_ + nb and ob_ < oa + na:
                if set(LIFE[a]) & set(LIFE[b]):
                    if {a, b} == {"rstd_own", "masks"} or {a, b} == {"nmr_own", "masks"}:
                        continue
                    if {a, b} & {"xnb_Q", "xfm_ow", "reps_o"} and {a, b} & {"x_t", "sq_j", "parts", "scol"}:
                        continue
                    raise AssertionError(f"sbuf overlap {a} / {b}")


def host_layout(inputs):
    Wq = np.ascontiguousarray(np.asarray(inputs["L0_attn_Wq"], dtype=np.float32))
    Wk = np.ascontiguousarray(np.asarray(inputs["L0_attn_Wk"], dtype=np.float32))
    Wv = np.ascontiguousarray(np.asarray(inputs["L0_attn_Wv"], dtype=np.float32))
    Wo = np.ascontiguousarray(np.asarray(inputs["L0_attn_Wo"], dtype=np.float32))
    eo = np.r_[0:64:2, 1:64:2]
    qrows = []
    for m in range(4):
        for i in range(4):
            for h in (8 * m + i, 8 * m + i + 4):
                qrows.extend((h * 64 + eo).tolist())
    wq_t = np.ascontiguousarray(Wq[np.array(qrows)].T)
    krows = []
    for m in range(4):
        for kv in (2 * m, 2 * m + 1):
            krows.extend((kv * 64 + eo).tolist())
    wk_t = np.ascontiguousarray(Wk[np.array(krows)].T)
    wv_t = np.ascontiguousarray(Wv.T)
    wo_t = np.ascontiguousarray(Wo.T)
    gate_t = np.ascontiguousarray(np.asarray(inputs["L0_ffn_gate_w"], dtype=np.float32).T)
    up_t = np.ascontiguousarray(np.asarray(inputs["L0_ffn_up_w"], dtype=np.float32).T)
    down_t = np.ascontiguousarray(np.asarray(inputs["L0_ffn_down_w"], dtype=np.float32).T)
    gate_s2 = np.ascontiguousarray(np.asarray(inputs["L0_ffn_gate_s"], dtype=np.float32).reshape(44, 128).T)
    up_s2 = np.ascontiguousarray(np.asarray(inputs["L0_ffn_up_s"], dtype=np.float32).reshape(44, 128).T)
    down_s2 = np.ascontiguousarray(np.asarray(inputs["L0_ffn_down_s"], dtype=np.float32).reshape(16, 128).T)

    inv_freq = (1.0 / 10000.0 ** (np.arange(0, 64, 2, dtype=np.float32) / 64)).astype(np.float32)
    t = np.arange(T, dtype=np.float32)
    freqs = np.einsum("i,j->ij", t, inv_freq)
    emb = np.concatenate([freqs, freqs], axis=-1)
    cos = np.cos(emb).astype(np.float32)
    sin = np.sin(emb).astype(np.float32)
    cosp = np.concatenate([cos[:, 0::2], cos[:, 1::2]], axis=1).T
    sinp = np.concatenate([-sin[:, 0::2].T, sin[:, 1::2].T], axis=0)
    cosf = np.ascontiguousarray(np.tile(cosp, (2, 1)))
    sinf = np.ascontiguousarray(np.tile(sinp, (2, 1)))

    jj = np.arange(512)
    masks = np.zeros((128, 4 * 512), np.float32)
    for k in range(4):
        masks[:, k * 512:(k + 1) * 512] = (
            jj[None, :] >= (128 * k + np.arange(128))[:, None]).astype(np.float32)
    masks = masks.astype(ml_dtypes.bfloat16)

    swap = np.r_[32:64, 0:32, 96:128, 64:96]
    permM = np.zeros((128, 128), np.float32)
    permM[swap, np.arange(128)] = 1.0
    permM = permM.astype(ml_dtypes.bfloat16)

    x = np.ascontiguousarray(np.asarray(inputs["L0_attn_x"], dtype=np.float32))
    shared = dict(wq_t=wq_t, wk_t=wk_t, wv_t=wv_t, wo_t=wo_t,
                  gate_t=gate_t, up_t=up_t, down_t=down_t,
                  gate_s2=gate_s2, up_s2=up_s2, down_s2=down_s2,
                  cosf=cosf, sinf=sinf, masks=masks, permM=permM,
                  identM=np.eye(128, dtype=np.float32))
    per_core = []
    for c in range(8):
        b, j = c // 4, c % 4
        own = np.r_[512 * j:512 * (j + 1), 512 * (7 - j):512 * (8 - j)]
        xb = x[b]
        d = dict(shared)
        d["x"] = xb
        d["xfm"] = np.ascontiguousarray(xb.T)
        d["xfm_own"] = np.ascontiguousarray(xb.T[:, own])
        d["x_own"] = np.ascontiguousarray(xb[own])
        d["cos_own"] = np.ascontiguousarray(cosf[:, own])
        d["sin_own"] = np.ascontiguousarray(sinf[:, own])
        per_core.append(d)
    return per_core


def build():
    _check_map()
    nc = bass.Bass("TRN2", target_bir_lowering=False)
    t_, v_, c_, g_, y_ = nc.tensor, nc.vector, nc.scalar, nc.gpsimd, nc.sync

    I = {}
    def din(name, shape, dtype=F32):
        I[name] = nc.dram_tensor(name, shape, dtype, kind="ExternalInput").ap()
    din("x", [T, HID]); din("xfm", [HID, T]); din("xfm_own", [HID, 1024])
    din("x_own", [1024, HID]); din("cos_own", [128, 1024]); din("sin_own", [128, 1024])
    din("wq_t", [HID, HID]); din("wk_t", [HID, 512]); din("wv_t", [HID, 512])
    din("wo_t", [HID, HID])
    din("gate_t", [HID, INTER]); din("up_t", [HID, INTER]); din("down_t", [INTER, HID])
    din("gate_s2", [128, 44]); din("up_s2", [128, 44]); din("down_s2", [128, 16])
    din("cosf", [128, T]); din("sinf", [128, T])
    din("masks", [128, 4 * 512], BF16); din("permM", [128, 128], BF16)
    din("identM", [128, 128])
    out = nc.dram_tensor("o", [1024, HID], F32, kind="ExternalOutput").ap()
    xn_dram = nc.dram_tensor("xn_dram", [HID, 1024], BF16, kind="Internal").ap()
    taps = {}
    A = {}
    for name, (off, shape, dt) in SB.items():
        A[name] = nc.alloc_sbuf_tensor_at(name, shape, dt, offset=off).ap()

    psA = nc.alloc_psum_tensor("psA", [128, 2048], F32).ap()
    psB = nc.alloc_psum_tensor("psB", [128, 2048], F32).ap()

    sem = {}
    C = {}
    for n in ["pe", "ac", "ve", "po", "s2", "ss", "sx", "scn", "sf", "sq2", "sr",
              "sw", "sxo", "sod", "sout", "gk", "gq", "ggu", "gdn", "gwo"]:
        sem[n] = nc.semaphore(n).__enter__()
        C[n] = 0

    def inc(instr, s, n=1):
        C[s] += n
        instr.then_inc(sem[s], n)
        return C[s]

    def W(eng, s, val=None):
        if val is None:
            val = C[s]
        if val > 0:
            eng.wait_ge(sem[s], val)

    def dma(q, s, out, in_):
        W(q, s)
        return inc(q.dma_start(out=out, in_=in_), s, 16)

    cnt = {}
    body = nc.body("main", valid_engines=list(mybir.ALL_ENGINES))
    body.__enter__()

    v_.memset(A["c_m8"], -8.0)
    v_.memset(A["c_eps"], 1e-5)
    v_.memset(A["ones64"], 1.0)
    v_.memset(A["ones128"], 1.0)
    vview = A["V_sb"].rearrange("p (s k d) -> p s k d", s=32, k=8)
    v_.memset(A["kbfA"], 0.0)
    v_.memset(A["kbfQ"], 0.0)
    ve_init = inc(v_.memset(vview[:, :, :, 64:65], 1.0), "ve")

    for nm, src in [("identM", "identM"), ("permM", "permM"),
                    ("gateS", "gate_s2"), ("upS", "up_s2"), ("downS", "down_s2")]:
        dma(y_, "ss", A[nm], I[src])
    sd_small = C["ss"]

    wkvv = A["wkv"].rearrange("p (h w) -> p h w", h=16)
    dma(g_, "gk", wkvv[:, :, 0:512], I["wk_t"].rearrange("(h p) o -> p h o", p=128))
    gw_wkv = dma(g_, "gk", wkvv[:, :, 512:1024],
                 I["wv_t"].rearrange("(h p) o -> p h o", p=128))

    def ln_stats(xtile, ti, rows_g, rows_n):
        sc = A["scol"][:, 8 * (ti % 2): 8 * (ti % 2) + 8]
        W(c_, "sx")
        W(c_, "ve")
        vv = inc(c_.activation(A["sq_j"], xtile, AF.Square, accum_out=sc[:, 1:2]), "ac")
        W(c_, "ac", vv)
        ac_sq = inc(c_.activation(A["sq_j"], xtile, AF.Identity, accum_out=sc[:, 0:1]),
                    "ac")
        W(v_, "ac", ac_sq)
        vv = inc(v_.tensor_scalar_mul(sc[:, 2:3], sc[:, 0:1], 1.0 / 2048.0), "ve")
        W(v_, "ve", vv)
        vv = inc(v_.tensor_mul(sc[:, 3:4], sc[:, 2:3], sc[:, 2:3]), "ve")
        W(v_, "ve", vv)
        ve_var = inc(v_.scalar_tensor_tensor(sc[:, 4:5], sc[:, 1:2], 1.0 / 2048.0,
                                             sc[:, 3:4], ALU.mult, ALU.subtract), "ve")
        W(c_, "ve", ve_var)
        vv = inc(c_.activation(sc[:, 5:6], sc[:, 4:5], AF.Ln, bias=A["c_eps"]), "ac")
        W(c_, "ac", vv)
        ac_rstd = inc(c_.activation(sc[:, 6:7], sc[:, 5:6], AF.Exp, scale=-0.5), "ac")
        W(v_, "ac", ac_rstd)
        ve_nmr = inc(v_.scalar_tensor_tensor(sc[:, 7:8], sc[:, 2:3], -1.0, sc[:, 6:7],
                                             ALU.mult, ALU.mult), "ve")
        W(t_, "ve", ve_nmr)
        W(t_, "ss", sd_small)
        t_.transpose(psA[0:1, 0:128], sc[:, 6:7], A["identM"])
        pe_tr = inc(t_.transpose(psA[0:1, 512:640], sc[:, 7:8], A["identM"]), "pe")
        W(v_, "pe", pe_tr)
        v_.tensor_copy(rows_g[0:1, ti * 128:(ti + 1) * 128], psA[0:1, 0:128])
        ve_row = inc(v_.tensor_copy(rows_n[0:1, ti * 128:(ti + 1) * 128],
                                    psA[0:1, 512:640]), "ve")
        return ve_row, ac_sq

    for ti in range(8):
        buf = A["x_t"][:, (ti % 2) * 2048:(ti % 2) * 2048 + 2048]
        if ti >= 2:
            W(y_, "ve", cnt[("row", "own", ti - 2)])
            W(y_, "ac", cnt[("sq", "own", ti - 2)])
        dma(y_, "sx", buf, I["x_own"][ti * 128:(ti + 1) * 128, :])
        r, s = ln_stats(buf, ti, A["rstd_own"], A["nmr_own"])
        cnt[("row", "own", ti)] = r
        cnt[("sq", "own", ti)] = s
    ve_rows_own = C["ve"]

    for tg in range(8):
        ve_rows = 0
        for ti in range(4):
            buf = A["x_t"][:, (ti % 2) * 2048:(ti % 2) * 2048 + 2048]
            prev = cnt.get(("row", tg, ti - 2), cnt.get(("row", tg - 1, ti + 2),
                           cnt.get(("row", "own", ti + 6), 0)))
            prevs = cnt.get(("sq", tg, ti - 2), cnt.get(("sq", tg - 1, ti + 2),
                            cnt.get(("sq", "own", ti + 6), 0)))
            W(y_, "ve", prev)
            W(y_, "ac", prevs)
            dma(y_, "sx", buf,
                I["x"][tg * 512 + ti * 128: tg * 512 + (ti + 1) * 128, :])
            r, s = ln_stats(buf, ti, A["rows_g"], A["rows_n"])
            cnt[("row", tg, ti)] = r
            cnt[("sq", tg, ti)] = s
            ve_rows = r
        W(y_, "ve", cnt.get(("kadd", tg - 1, 3), 0))
        dma(y_, "scn", A["csn"][:, 0:512], I["cosf"][:, tg * 512:(tg + 1) * 512])
        dma(y_, "scn", A["csn"][:, 512:1024], I["sinf"][:, tg * 512:(tg + 1) * 512])
        sd_csn = C["scn"]
        W(t_, "ve", ve_rows)
        t_.matmul(psA[:, 1024:1536], A["ones128"], A["rows_g"], start=True, stop=True)
        pe_rep = inc(t_.matmul(psA[:, 1536:2048], A["ones128"], A["rows_n"],
                               start=True, stop=True), "pe")
        W(v_, "pe", pe_rep)
        W(v_, "po", cnt.get(("app", tg - 1, NHC - 1), 0))
        v_.tensor_copy(A["reps"][:, 0:512], psA[:, 1024:1536])
        ve_rep = inc(v_.tensor_copy(A["reps"][:, 512:1024], psA[:, 1536:2048]), "ve")
        xnr = A["xn_rot"] if tg % 2 == 0 else A["xn_rot2"]
        for h in range(NHC):
            fb = A["xfm_t"][:, (h % 2) * 512:(h % 2) * 512 + 512]
            prev = cnt.get(("appm", tg, h - 2), cnt.get(("appm", tg - 1, 14 + h), 0))
            W(y_, "ve", prev)
            dma(y_, "sf", fb, I["xfm"][h * 128:(h + 1) * 128, tg * 512:(tg + 1) * 512])
            W(v_, "sf")
            if h == 0:
                W(v_, "ve", ve_rep)
            ab = A["appt"][:, (h % 2) * 512:(h % 2) * 512 + 512]
            if h >= 2:
                W(v_, "po", cnt.get(("app", tg, h - 2), 0))
            cnt[("appm", tg, h)] = inc(v_.tensor_mul(ab, fb, A["reps"][:, 0:512]), "ve")
            W(g_, "ve", cnt[("appm", tg, h)])
            if h == 0:
                W(g_, "pe", cnt.get(("proj", tg - 2), 0))
            cnt[("app", tg, h)] = inc(
                g_.tensor_add(xnr[:, h * 512:(h + 1) * 512], ab,
                              A["reps"][:, 512:1024]), "po")
        W(t_, "gk", gw_wkv)
        W(t_, "po", cnt[("app", tg, NHC - 1)])
        W(t_, "ve", cnt.get(("kt1", tg - 1, 3), 0))
        for h in range(NHC):
            for ck in range(4):
                i = t_.matmul(psB[:, ck * 512:(ck + 1) * 512],
                              A["wkv"][:, h * 1024 + ck * 128: h * 1024 + (ck + 1) * 128],
                              xnr[:, h * 512:(h + 1) * 512],
                              start=(h == 0), stop=(h == NHC - 1))
        pe_K = inc(i, "pe")
        for ck in range(4):
            kb = A["kbfA"][:, (ck % 2) * 512:(ck % 2) * 512 + 512]
            W(v_, "pe", pe_K)
            W(v_, "pe", cnt.get(("ksw", tg, ck - 2), 0))
            ve_kb = inc(v_.tensor_copy(kb, psB[:, ck * 512:(ck + 1) * 512]), "ve")
            W(t_, "ve", ve_kb)
            W(t_, "ve", cnt.get(("kadd", tg, ck - 2), 0))
            cnt[("ksw", tg, ck)] = inc(
                t_.matmul(psA[:, (ck % 2) * 512:(ck % 2) * 512 + 512], A["permM"], kb,
                          start=True, stop=True), "pe")
            W(v_, "scn", sd_csn)
            W(v_, "pe", cnt[("ksw", tg, ck)])
            tb = A["t1A"][:, (ck % 2) * 512:(ck % 2) * 512 + 512]
            cnt[("kt1", tg, ck)] = inc(
                v_.tensor_mul(tb, psB[:, ck * 512:(ck + 1) * 512], A["csn"][:, 0:512]), "ve")
            vv = inc(v_.tensor_mul(psA[:, (2 + ck % 2) * 512:(2 + ck % 2) * 512 + 512],
                                   psA[:, (ck % 2) * 512:(ck % 2) * 512 + 512],
                                   A["csn"][:, 512:1024]), "ve")
            W(v_, "ve", vv)
            cnt[("kadd", tg, ck)] = inc(
                v_.tensor_add(A["K_sb"][:, ck * 4096 + tg * 512: ck * 4096 + (tg + 1) * 512],
                              tb, psA[:, (2 + ck % 2) * 512:(2 + ck % 2) * 512 + 512]), "ve")
        W(t_, "ve", cnt[("kadd", tg, 3)])
        W(t_, "ve", cnt.get(("vev", tg - 1, 3), 0))
        for h in range(NHC):
            for tv in range(4):
                i = t_.matmul(psA[:, tv * 512:(tv + 1) * 512],
                              xnr[:, h * 512 + tv * 128: h * 512 + (tv + 1) * 128],
                              A["wkv"][:, h * 1024 + 512: h * 1024 + 1024],
                              start=(h == 0), stop=(h == NHC - 1))
        pe_V = inc(i, "pe")
        cnt[("proj", tg)] = C["pe"]
        W(v_, "pe", pe_V)
        if tg == 0:
            W(v_, "ve", ve_init)
        for tv in range(4):
            dst = vview[:, tg * 4 + tv, :, 0:64]
            cnt[("vev", tg, tv)] = inc(
                v_.tensor_copy(dst, psA[:, tv * 512:(tv + 1) * 512]
                               .rearrange("p (k d) -> p k d", k=8)), "ve")

    ve_A_end = C["ve"]

    W(t_, "ve", ve_A_end)
    for half in range(2):
        t_.matmul(psA[:, half * 512:(half + 1) * 512], A["ones128"],
                  A["rstd_own"][0:1, half * 512:(half + 1) * 512], start=True, stop=True)
        i = t_.matmul(psA[:, 1024 + half * 512: 1024 + (half + 1) * 512], A["ones128"],
                      A["nmr_own"][0:1, half * 512:(half + 1) * 512], start=True, stop=True)
    pe_orep = inc(i, "pe")
    W(v_, "pe", pe_orep)
    v_.tensor_copy(A["reps_o"][:, 0:1024], psA[:, 0:1024])
    ve_orep = inc(v_.tensor_copy(A["reps_o"][:, 1024:2048], psA[:, 1024:2048]), "ve")

    for h in range(NHC):
        fb = A["xfm_ow"][:, (h % 2) * 1024:(h % 2) * 1024 + 1024]
        if h < 2:
            W(y_, "ve", cnt[("row", 7, 3)])
            W(y_, "ac", cnt[("sq", 7, 3)])
        W(y_, "ve", cnt.get(("oxn", h - 2), 0))
        dma(y_, "sf", fb, I["xfm_own"][h * 128:(h + 1) * 128, :])
        W(v_, "sf")
        if h == 0:
            W(v_, "ve", ve_orep)
        W(v_, "sw", cnt.get(("oxnd", h - 1), 0))
        vv = inc(v_.tensor_mul(A["t1Q"], fb, A["reps_o"][:, 0:1024]), "ve")
        W(v_, "ve", vv)
        cnt[("oxn", h)] = inc(v_.tensor_add(A["xnb_Q"][:, 0:1024], A["t1Q"],
                                            A["reps_o"][:, 1024:2048]), "ve")
        W(y_, "ve", cnt[("oxn", h)])
        dma(y_, "sw", xn_dram[h * 128:(h + 1) * 128, :], A["xnb_Q"][:, 0:1024])
        cnt[("oxnd", h)] = C["sw"]
    ou_xn = C["sw"]

    W(y_, "ve", cnt[("kadd", 7, 3)])
    dma(y_, "sq2", A["cosownQ"], I["cos_own"])
    dma(y_, "sq2", A["sinownQ"], I["sin_own"])
    sd_csown = C["sq2"]
    W(y_, "ve", cnt[("oxn", NHC - 1)])
    dma(y_, "sq2", A["masks"], I["masks"])
    sd_masks = C["sq2"]

    wqv = A["wq_st"].rearrange("p (b h w) -> p b h w", b=2, h=16)
    for op8 in range(8):
        if op8 < 2:
            W(g_, "pe", cnt[("proj", 7)])
        W(g_, "pe", cnt.get(("qmm", op8 - 2, NHC - 1), 0))
        gw_wq = dma(g_, "gq", wqv[:, op8 % 2],
                    I["wq_t"].rearrange("(h p) o -> p h o", p=128)[:, :, op8 * 256:(op8 + 1) * 256])
        for h in range(NHC):
            xb = A["xnb_Q"][:, (1 + h % 2) * 1024:(1 + h % 2) * 1024 + 1024]
            prev = cnt.get(("qmm", op8, h - 2), cnt.get(("qmm", op8 - 1, 14 + h), 0))
            W(y_, "pe", prev)
            W(y_, "sw", 16 * (h + 1))
            dma(y_, "sr", xb, xn_dram[h * 128:(h + 1) * 128, :])
            W(t_, "sr")
            if h == 0:
                W(t_, "gq", gw_wq)
                W(t_, "ve", cnt.get(("qrope", op8 - 1), ve_orep))
            for oc in range(2):
                for th in range(2):
                    i = t_.matmul(psA[:, (oc * 2 + th) * 512:(oc * 2 + th + 1) * 512],
                                  wqv[:, op8 % 2, h, oc * 128:(oc + 1) * 128],
                                  xb[:, th * 512:(th + 1) * 512],
                                  start=(h == 0), stop=(h == NHC - 1))
            cnt[("qmm", op8, h)] = inc(i, "pe")
        pe_q = cnt[("qmm", op8, NHC - 1)]
        for oc in range(2):
            ch = op8 * 2 + oc
            W(v_, "pe", pe_q)
            W(v_, "pe", cnt.get(("qsw", ch - 1), 0))
            ve_kb = inc(v_.tensor_copy(A["kbfQ"], psA[:, oc * 1024:(oc + 1) * 1024]), "ve")
            W(t_, "ve", ve_kb)
            if ch == 0:
                W(t_, "ve", ve_A_end)
            else:
                W(t_, "ve", cnt.get(("qadd", ch - 1), 0))
            t_.matmul(psB[:, 0:512], A["permM"], A["kbfQ"][:, 0:512], start=True, stop=True)
            cnt[("qsw", ch)] = inc(
                t_.matmul(psB[:, 512:1024], A["permM"], A["kbfQ"][:, 512:1024],
                          start=True, stop=True), "pe")
            W(v_, "sq2", sd_csown)
            W(v_, "pe", cnt[("qsw", ch)])
            vv = inc(v_.tensor_mul(A["t1Q"], psA[:, oc * 1024:(oc + 1) * 1024],
                                   A["cosownQ"]), "ve")
            vv = inc(v_.tensor_mul(psB[:, 1024:2048], psB[:, 0:1024], A["sinownQ"]), "ve")
            W(v_, "ve", vv)
            cnt[("qadd", ch)] = inc(
                v_.tensor_add(A["Q_sb"][:, ch * 1024:(ch + 1) * 1024], A["t1Q"],
                              psB[:, 1024:2048]), "ve")
        cnt[("qrope", op8)] = C["ve"]

    ve_Q_end = C["ve"]
    pe_Q_end = C["pe"]
    ac_Q_end = C["ac"]
    pid_t = t_.compute_val(t_.partition_id() % 4)
    pid_c = c_.compute_val(c_.partition_id() % 4)
    pid_v = v_.compute_val(v_.partition_id() % 4)
    body.__exit__(None, None, None)

    snap = dict(C)
    final_counts = None
    for var in nc.Switch(engines=[t_, c_, v_], index=[pid_t, pid_c, pid_v], n=4):
        C.clear(); C.update(snap)
        j = var
        nodd = 0
        cnt_s2 = [0]
        first = True
        vmask_first = True
        exp_cnt = {}
        for gi, g in enumerate([j, 7 - j]):
            nsb = 4 * (g + 1)
            for ch in range(16):
                m = ch // 4
                av_free = C["ve"]
                pe_av = 0
                for sb in range(nsb):
                    par = sb % 2
                    if first:
                        W(t_, "ve", ve_Q_end)
                        W(t_, "ac", ac_Q_end)
                        first = False
                    if (ch, sb - 2) in exp_cnt or sb >= 2:
                        W(t_, "ac", exp_cnt.get((ch, sb - 2), 0))
                    elif sb == 0 or sb == 1:
                        W(t_, "ac")
                    for hh in range(2):
                        b0 = 64 * hh
                        i = t_.matmul(
                            psA[:, par * 1024 + hh * 512: par * 1024 + (hh + 1) * 512],
                            A["K_sb"][b0:b0 + 64,
                                      m * 4096 + sb * 128: m * 4096 + (sb + 1) * 128],
                            A["Q_sb"][b0:b0 + 64,
                                      ch * 1024 + gi * 512: ch * 1024 + (gi + 1) * 512],
                            start=True, stop=True)
                    pe_qk = inc(i, "pe")
                    W(c_, "pe", pe_qk)
                    eb = A["Ebuf"][:, (sb % 2) * 1024:(sb % 2) * 1024 + 1024]
                    ac_exp = inc(c_.activation(eb, psA[:, par * 1024:(par + 1) * 1024],
                                               AF.Exp, bias=A["c_m8"], scale=0.125), "ac")
                    exp_cnt[(ch, sb)] = ac_exp
                    diag = sb >= 4 * g
                    if diag:
                        k = sb - 4 * g
                        if vmask_first:
                            W(v_, "sq2", sd_masks)
                            vmask_first = False
                        W(v_, "ac", ac_exp)
                        W(v_, "pe")
                        for hh in range(2):
                            i = v_.tensor_mul(A["Emask"][:, hh * 512:(hh + 1) * 512],
                                              eb[:, hh * 512:(hh + 1) * 512],
                                              A["masks"][:, k * 512:(k + 1) * 512])
                        ve_mask = inc(i, "ve")
                        W(t_, "ve", ve_mask)
                        src = A["Emask"]
                    else:
                        W(t_, "ac", ac_exp)
                        src = eb
                    if sb == 0:
                        W(t_, "ve", av_free)
                        W(t_, "s2", cnt_s2[0])
                    for hh in range(2):
                        kv = 2 * m + hh
                        i = t_.matmul(
                            psB[0:65, hh * 512:(hh + 1) * 512],
                            A["V_sb"].rearrange("p (s w) -> p s w", s=32)
                            [:, sb, kv * 65:(kv + 1) * 65],
                            src[:, hh * 512:(hh + 1) * 512],
                            start=(sb == 0), stop=(sb == nsb - 1))
                    pe_av = inc(i, "pe")
                for hh in range(2):
                    head = 8 * (ch // 4) + (ch % 4) + 4 * hh
                    W(v_, "pe")
                    ve_rd = inc(v_.reciprocal(A["rdbuf"],
                                              psB[64:65, hh * 512:(hh + 1) * 512]), "ve")
                    W(t_, "ve", ve_rd)
                    i = t_.matmul(psA[0:64, 1024 + hh * 512: 1024 + hh * 512 + 512]
                                  if False else
                                  psB[0:64, 1024 + hh * 512: 1024 + hh * 512 + 512],
                                  A["ones64"], A["rdbuf"], start=True, stop=True)
                    pe_bc = inc(i, "pe")
                    bcb = A["bcbuf0"] if hh == 0 else A["bcbuf1"]
                    W(v_, "pe", pe_bc)
                    ve_bc = inc(v_.tensor_copy(bcb, psB[0:64, 1024 + hh * 512:
                                                        1024 + hh * 512 + 512]), "ve")
                    W(v_, "ve", ve_bc)
                    dst_cols = slice((head // 2) * 1024 + gi * 512,
                                     (head // 2) * 1024 + (gi + 1) * 512)
                    if head % 2 == 0:
                        i = v_.tensor_mul(A["A_sb"][0:64, dst_cols],
                                          psB[0:64, hh * 512:(hh + 1) * 512], bcb)
                        inc(i, "ve")
                    else:
                        par2 = nodd % 2
                        W(v_, "s2", cnt_s2[0] - 16 if nodd >= 2 else 0)
                        i = v_.tensor_mul(A["antmp"][:, par2 * 512:(par2 + 1) * 512],
                                          psB[0:64, hh * 512:(hh + 1) * 512], bcb)
                        ve_an = inc(i, "ve")
                        W(c_, "ve", ve_an)
                        W(c_, "s2")
                        inc(c_.dma_start(out=A["A_sb"][64:128, dst_cols],
                                         in_=A["antmp"][:, par2 * 512:(par2 + 1) * 512]),
                            "s2", 16)
                        cnt_s2[0] = C["s2"]
                        nodd += 1
        if final_counts is None:
            final_counts = dict(C)
        else:
            assert final_counts == C, "switch variant sem divergence"
    C.clear(); C.update(final_counts)

    body2 = nc.body("tail", valid_engines=list(mybir.ALL_ENGINES))
    body2.__enter__()
    ve_B_end = C["ve"]
    pe_B_end = C["pe"]
    ac_B_end = C["ac"]
    s2_B_end = C["s2"]

    for h in range(NHC):
        if h == 0:
            W(y_, "pe", pe_Q_end)
            W(y_, "ve", ve_Q_end)
        dma(y_, "sxo", A["xn_own"][:, h * 1024:(h + 1) * 1024],
            xn_dram[h * 128:(h + 1) * 128, :])
    sd_xno = C["sxo"]

    guv_g = A["gu_g"].rearrange("p (b h w) -> p b h w", b=2, h=16)
    guv_u = A["gu_u"].rearrange("p (b h w) -> p b h w", b=2, h=16)
    gatev = I["gate_t"].rearrange("(h p) i -> p h i", p=128)
    upv = I["up_t"].rearrange("(h p) i -> p h i", p=128)
    for ic in range(NIC):
        if ic < 2:
            W(g_, "ve", ve_B_end)
        W(g_, "pe", cnt.get(("fmm", ic - 2, NHC - 1), 0))
        dma(g_, "ggu", guv_g[:, ic % 2], gatev[:, :, ic * 128:(ic + 1) * 128])
        gw_gu = dma(g_, "ggu", guv_u[:, ic % 2], upv[:, :, ic * 128:(ic + 1) * 128])
        ps = psA if ic % 2 == 0 else psB
        for h in range(NHC):
            if h == 0:
                W(t_, "sxo", sd_xno)
                W(t_, "ggu", gw_gu)
                if ic < 2:
                    W(t_, "ve", ve_B_end)
                    W(t_, "ac", ac_B_end)
                else:
                    W(t_, "ac", cnt.get(("fsil", ic - 2), 0))
                    W(t_, "ve", cnt.get(("fH", ic - 2), 0))
            for th in range(2):
                t_.matmul(ps[:, th * 512:(th + 1) * 512],
                          guv_g[:, ic % 2, h, :],
                          A["xn_own"][:, h * 1024 + th * 512: h * 1024 + (th + 1) * 512],
                          start=(h == 0), stop=(h == NHC - 1))
            for th in range(2):
                i = t_.matmul(ps[:, 1024 + th * 512: 1024 + (th + 1) * 512],
                              guv_u[:, ic % 2, h, :],
                              A["xn_own"][:, h * 1024 + th * 512: h * 1024 + (th + 1) * 512],
                              start=(h == 0), stop=(h == NHC - 1))
        cnt[("fmm", ic, NHC - 1)] = inc(i, "pe")
        W(c_, "pe", cnt[("fmm", ic, NHC - 1)])
        W(c_, "ve", cnt.get(("fH", ic - 1), 0))
        cnt[("fsil", ic)] = inc(
            c_.activation(A["gtC"], ps[:, 0:1024], AF.Silu,
                          scale=A["gateS"][:, ic:ic + 1]), "ac")
        W(v_, "ac", cnt[("fsil", ic)])
        if ic == 0:
            W(v_, "s2", s2_B_end)
        cnt[("fH", ic)] = inc(
            v_.scalar_tensor_tensor(A["H_sb"][:, ic * 1024:(ic + 1) * 1024],
                                    ps[:, 1024:2048], A["upS"][:, ic:ic + 1], A["gtC"],
                                    ALU.mult, ALU.mult), "ve")
    ve_C_end = C["ve"]
    pe_C_end = C["pe"]

    dnv = A["dn_st"].rearrange("p (b i w) -> p b i w", b=2, i=22)
    wov = A["wo_st"].rearrange("p (b h w) -> p b h w", b=2, h=16)
    downv = I["down_t"].rearrange("(i p) o -> p i o", p=128)
    wodv = I["wo_t"].rearrange("(h p) o -> p h o", p=128)
    for ob in range(8):
        for half in range(2):
            if ob == 0:
                W(g_, "pe", pe_C_end)
            W(g_, "pe", cnt.get(("dmm", ob - 1, half), 0))
            cnt[("gwdn", ob, half)] = dma(
                g_, "gdn", dnv[:, half],
                downv[:, half * 22:(half + 1) * 22, ob * 256:(ob + 1) * 256])
        if ob == 0:
            W(g_, "pe", pe_C_end)
        W(g_, "pe", cnt.get(("wmm", ob - 2), 0))
        cnt[("gwwo", ob)] = dma(g_, "gwo", wov[:, ob % 2],
                                wodv[:, :, ob * 256:(ob + 1) * 256])
        for i44 in range(NIC):
            half, idx = (0, i44) if i44 < 22 else (1, i44 - 22)
            if i44 == 0:
                W(t_, "ve", ve_C_end)
                W(t_, "gdn", cnt[("gwdn", ob, 0)])
                W(t_, "ac", cnt.get(("dg", ob - 1, 1), 0))
                W(t_, "ve", cnt.get(("dadd", ob - 1, 1, 7), 0))
            if i44 == 22:
                W(t_, "gdn", cnt[("gwdn", ob, 1)])
            for oc in range(2):
                for th in range(2):
                    i = t_.matmul(psA[:, (oc * 2 + th) * 512:(oc * 2 + th + 1) * 512],
                                  dnv[:, half, idx, oc * 128:(oc + 1) * 128],
                                  A["H_sb"][:, i44 * 1024 + th * 512:
                                            i44 * 1024 + (th + 1) * 512],
                                  start=(i44 == 0), stop=(i44 == NIC - 1))
            if i44 == 21:
                cnt[("dmm", ob, 0)] = inc(i, "pe")
        cnt[("dmm", ob, 1)] = pe_dn = inc(i, "pe")
        for h in range(NHC):
            if h == 0:
                W(t_, "gwo", cnt[("gwwo", ob)])
                W(t_, "ve", ve_B_end)
                W(t_, "s2", s2_B_end)
                W(t_, "ve", cnt.get(("dsg", ob - 1, 1), 0))
            for oc in range(2):
                for th in range(2):
                    i = t_.matmul(psB[:, (oc * 2 + th) * 512:(oc * 2 + th + 1) * 512],
                                  wov[:, ob % 2, h, oc * 128:(oc + 1) * 128],
                                  A["A_sb"][:, h * 1024 + th * 512:
                                            h * 1024 + (th + 1) * 512],
                                  start=(h == 0), stop=(h == NHC - 1))
        cnt[("wmm", ob)] = pe_wo = inc(i, "pe")
        for oc in range(2):
            occ = 2 * ob + oc
            W(c_, "pe", pe_dn)
            W(c_, "ve", cnt.get(("dsg", ob, oc - 1), cnt.get(("dsg", ob - 1, 1), 0)))
            cnt[("dg", ob, oc)] = inc(
                c_.activation(A["g_D"], psA[:, oc * 1024:(oc + 1) * 1024], AF.Silu,
                              scale=A["downS"][:, occ:occ + 1]), "ac")
            W(v_, "pe", pe_wo)
            W(v_, "ac", cnt[("dg", ob, oc)])
            cnt[("dsg", ob, oc)] = inc(
                v_.tensor_mul(A["sg_D"][:, oc * 1024:(oc + 1) * 1024],
                              psB[:, oc * 1024:(oc + 1) * 1024], A["g_D"]), "ve")
        for tt in range(8):
            xo = A["xo_D"][:, (tt % 2) * 256:(tt % 2) * 256 + 256]
            if (ob, tt) == (0, 0):
                W(y_, "pe", pe_C_end)
            W(y_, "ve", cnt.get(("dadd", ob, 1, tt - 2),
                                cnt.get(("dadd", ob - 1, 1, 6 + tt), 0)))
            dma(y_, "sod", xo, I["x_own"][tt * 128:(tt + 1) * 128,
                                           ob * 256:(ob + 1) * 256])
            sd_xo = C["sod"]
            for oc in range(2):
                occ = 2 * ob + oc
                W(t_, "ve", cnt[("dsg", ob, 1)])
                W(t_, "ac", cnt[("dg", ob, 1)])
                W(t_, "ve", cnt.get(("dadd", ob, oc, tt - 2),
                                    cnt.get(("dadd", ob - 1, oc, 6 + tt), 0)))
                bank = 2 * (tt % 2) + oc
                i = t_.transpose(psA[:, bank * 512: bank * 512 + 128],
                                 A["sg_D"][:, oc * 1024 + tt * 128:
                                           oc * 1024 + (tt + 1) * 128], A["identM"])
                pe_tr = inc(i, "pe")
                W(v_, "pe", pe_tr)
                W(v_, "sod", sd_xo)
                W(v_, "sout", cnt.get(("dout", ob, tt - 4),
                                      cnt.get(("dout", ob - 1, 4 + tt), 0)))
                cnt[("dadd", ob, oc, tt)] = inc(
                    v_.tensor_add(A["out_D"][:, (tt % 4) * 256 + oc * 128:
                                             (tt % 4) * 256 + oc * 128 + 128],
                                  psA[:, bank * 512: bank * 512 + 128],
                                  xo[:, oc * 128:(oc + 1) * 128]), "ve")
            W(y_, "ve", cnt[("dadd", ob, 1, tt)])
            dma(y_, "sout", out[tt * 128:(tt + 1) * 128, ob * 256:(ob + 1) * 256],
                A["out_D"][:, (tt % 4) * 256:(tt % 4) * 256 + 256])
            cnt[("dout", ob, tt)] = C["sout"]

    W(y_, "sout", C["sout"])
    body2.__exit__(None, None, None)
    return nc


def kernel(**inputs):
    per_core = host_layout(inputs)
    nc = build()
    res = run_bass_kernel_spmd(nc, per_core, core_ids=list(range(8)))
    outp = np.zeros((2, 4096, 2048), np.float32)
    for c in range(8):
        b, j = c // 4, c % 4
        r = np.asarray(res.results[c]["o"], np.float32)
        outp[b, 512 * j:512 * (j + 1)] = r[0:512]
        outp[b, 512 * (7 - j):512 * (8 - j)] = r[512:1024]
    return outp


if __name__ == "__main__":
    pass


# version 06
# speedup vs baseline: 1.2810x; 1.2810x over previous
import numpy as np
import ml_dtypes
import concourse.bass as bass
import concourse.mybir as mybir
from concourse.bass_utils import run_bass_kernel_spmd

F32 = mybir.dt.float32
BF16 = mybir.dt.bfloat16
AF = mybir.ActivationFunctionType
ALU = mybir.AluOpType
AX = mybir.AxisListType

T, HID, INTER = 4096, 2048, 5632
NHC, NIC = 16, 44
KB = 1024

SB = {}
SB_BASE = 13 * KB
def _sb(name, off_kb, shape, dtype):
    assert name not in SB
    off = int(off_kb * KB) + SB_BASE
    off = (off + 31) // 32 * 32
    SB[name] = (off, shape, dtype)

_sb("c_m8",     4.000, [128, 1], F32)
_sb("c_eps",    4.0625, [128, 1], F32)
_sb("ones64",   4.125, [1, 64], F32)
_sb("ones128",  4.375, [1, 128], F32)
_sb("gateS",    4.875, [128, 44], F32)
_sb("upS",      5.0625, [128, 44], F32)
_sb("downS",    5.25, [128, 16], F32)
_sb("permM",    5.3125, [128, 128], BF16)
_sb("identM",   5.5625, [128, 128], F32)
_sb("rstd_own", 6.25, [1, 1024], F32)
_sb("nmr_own", 10.25, [1, 1024], F32)
_sb("masks",    14.25, [128, 4 * 512], BF16)
_sb("gu_g",   110.0, [128, 2 * 16 * 128], BF16)
_sb("gu_u",   118.0, [128, 2 * 16 * 128], BF16)
_sb("K_sb",    22.0, [128, 4 * 4096], BF16)
_sb("V_sb",    54.0, [128, 32 * 520], BF16)
_sb("H_sb",    22.0, [128, 44 * 1024], BF16)
_sb("Emask",   88.0, [128, 2048], BF16)
_sb("gtC",    132.5, [128, 1024], F32)
_sb("Ebuf",    92.5, [128, 2 * 2048], BF16)
_sb("rdbuf",  101.0, [1, 512], F32)
_sb("bcbuf0", 103.5, [64, 512], F32)
_sb("bcbuf1", 105.5, [64, 512], F32)
_sb("antmp",  107.5, [64, 2 * 512], BF16)
_sb("xn_rot", 110.0, [128, 16 * 512], BF16)
_sb("xn_rot2", 92.5, [128, 16 * 512], BF16)
_sb("wkv",    126.0, [128, 16 * 1024], BF16)
_sb("Q_sb",   110.0, [128, 16 * 1024], BF16)
_sb("dn_st",  110.0, [128, 2 * 22 * 256], BF16)
_sb("csn",    158.0, [128, 2 * 512], F32)
_sb("xfm_t",  162.0, [128, 2 * 1024], F32)
_sb("kbfA",   170.0, [128, 2 * 512], BF16)
_sb("t1A",    172.0, [128, 2 * 512], F32)
_sb("appt",   176.0, [128, 2 * 512], F32)
_sb("wq_st",  142.5, [128, 2 * 16 * 256], BF16)
_sb("cosownQ", 88.0, [128, 1024], F32)
_sb("sinownQ", 92.5, [128, 1024], F32)
_sb("kbfQ",    96.5, [128, 1024], BF16)
_sb("t1Q",    158.6, [128, 1024], F32)
_sb("xn_own", 142.5, [128, 16 * 1024], BF16)
_sb("wo_st",  142.5, [128, 2 * 16 * 256], BF16)
_sb("g_D",    158.5, [128, 1024], F32)
_sb("sg_D",   162.5, [128, 2 * 1024], F32)
_sb("xo_D",   170.5, [128, 2 * 256], F32)
_sb("out_D",  172.5, [128, 4 * 256], F32)
_sb("x_t",    180.5, [128, 2 * 2048], F32)
_sb("sq_j",   196.5, [128, 2048], BF16)
_sb("parts",  200.6, [128, 16], F32)
_sb("scol",   200.7, [128, 16], F32)
_sb("rows_g", 200.8, [1, 512], F32)
_sb("rows_n", 202.8, [1, 512], F32)
_sb("reps",   204.8, [128, 2 * 512], F32)
_sb("xnb_Q",  176.5, [128, 3 * 1024], BF16)
_sb("xfm_ow", 182.5, [128, 2 * 1024], F32)
_sb("reps_o", 190.5, [128, 2 * 1024], F32)
_sb("A_sb",   174.5, [128, 16 * 1024], BF16)
SB["out_D"] = (int(170.5 * KB) + SB_BASE, [128, 4 * 256], F32)
SB["xo_D"] = (int(136.5 * KB) + SB_BASE, [128, 2 * 256], F32)
SB_LIMIT = 229 * KB


def _esz(dt):
    return 2 if dt == BF16 else 4


def _check_map():
    for name, (off, shape, dt) in SB.items():
        n = 1
        for s in shape[1:]:
            n *= s
        assert off + n * _esz(dt) <= SB_LIMIT, f"{name} out of sbuf"
    LIFE = {
        "c_m8": "WAQBCD", "c_eps": "WAQBCD", "ones64": "WAQBCD", "ones128": "WAQBCD",
        "gateS": "WAQBCD", "upS": "WAQBCD", "downS": "WAQBCD", "permM": "WAQBCD",
        "identM": "WAQBCD",
        "rstd_own": "AQ", "nmr_own": "AQ", "masks": "B", "gu_g": "C", "gu_u": "C",
        "K_sb": "AQB", "V_sb": "AQB", "H_sb": "CD",
        "Emask": "B", "gtC": "C", "Ebuf": "B", "rdbuf": "B", "bcbuf0": "B", "bcbuf1": "B", "antmp": "B",
        "xn_rot": "A", "xn_rot2": "A", "wkv": "A", "Q_sb": "QB", "dn_st": "D",
        "csn": "A", "xfm_t": "A", "kbfA": "A", "t1A": "A", "appt": "A",
        "wq_st": "Q", "cosownQ": "Q", "sinownQ": "Q", "kbfQ": "Q", "t1Q": "Q",
        "xn_own": "C", "wo_st": "D", "g_D": "D", "sg_D": "D", "xo_D": "D", "out_D": "D",
        "x_t": "AQ", "sq_j": "AQ", "parts": "AQ", "scol": "AQ",
        "rows_g": "A", "rows_n": "A", "reps": "A",
        "xnb_Q": "Q", "xfm_ow": "Q", "reps_o": "Q", "A_sb": "BCD",
    }
    LIFE["masks"] = "QB"
    names = list(SB)
    for i, a in enumerate(names):
        for b in names[i + 1:]:
            (oa, sa, da), (ob_, sb_, db) = SB[a], SB[b]
            na = _esz(da)
            for s in sa[1:]:
                na *= s
            nb = _esz(db)
            for s in sb_[1:]:
                nb *= s
            if oa < ob_ + nb and ob_ < oa + na:
                if set(LIFE[a]) & set(LIFE[b]):
                    if {a, b} == {"rstd_own", "masks"} or {a, b} == {"nmr_own", "masks"}:
                        continue
                    if {a, b} & {"xnb_Q", "xfm_ow", "reps_o"} and {a, b} & {"x_t", "sq_j", "parts", "scol"}:
                        continue
                    raise AssertionError(f"sbuf overlap {a} / {b}")


def host_layout(inputs):
    Wq = np.ascontiguousarray(np.asarray(inputs["L0_attn_Wq"], dtype=np.float32))
    Wk = np.ascontiguousarray(np.asarray(inputs["L0_attn_Wk"], dtype=np.float32))
    Wv = np.ascontiguousarray(np.asarray(inputs["L0_attn_Wv"], dtype=np.float32))
    Wo = np.ascontiguousarray(np.asarray(inputs["L0_attn_Wo"], dtype=np.float32))
    eo = np.r_[0:64:2, 1:64:2]
    qrows = []
    for m in range(4):
        for i in range(4):
            for h in (8 * m + i, 8 * m + i + 4):
                qrows.extend((h * 64 + eo).tolist())
    wq_t = np.ascontiguousarray(Wq[np.array(qrows)].T)
    krows = []
    for m in range(4):
        for kv in (2 * m, 2 * m + 1):
            krows.extend((kv * 64 + eo).tolist())
    wk_t = np.ascontiguousarray(Wk[np.array(krows)].T)
    wv_t = np.ascontiguousarray(Wv.T)
    wo_t = np.ascontiguousarray(Wo.T)
    gate_t = np.ascontiguousarray(np.asarray(inputs["L0_ffn_gate_w"], dtype=np.float32).T)
    up_t = np.ascontiguousarray(np.asarray(inputs["L0_ffn_up_w"], dtype=np.float32).T)
    down_t = np.ascontiguousarray(np.asarray(inputs["L0_ffn_down_w"], dtype=np.float32).T)
    gate_s2 = np.ascontiguousarray(np.asarray(inputs["L0_ffn_gate_s"], dtype=np.float32).reshape(44, 128).T)
    up_s2 = np.ascontiguousarray(np.asarray(inputs["L0_ffn_up_s"], dtype=np.float32).reshape(44, 128).T)
    down_s2 = np.ascontiguousarray(np.asarray(inputs["L0_ffn_down_s"], dtype=np.float32).reshape(16, 128).T)

    inv_freq = (1.0 / 10000.0 ** (np.arange(0, 64, 2, dtype=np.float32) / 64)).astype(np.float32)
    t = np.arange(T, dtype=np.float32)
    freqs = np.einsum("i,j->ij", t, inv_freq)
    emb = np.concatenate([freqs, freqs], axis=-1)
    cos = np.cos(emb).astype(np.float32)
    sin = np.sin(emb).astype(np.float32)
    cosp = np.concatenate([cos[:, 0::2], cos[:, 1::2]], axis=1).T
    sinp = np.concatenate([-sin[:, 0::2].T, sin[:, 1::2].T], axis=0)
    cosf = np.ascontiguousarray(np.tile(cosp, (2, 1)))
    sinf = np.ascontiguousarray(np.tile(sinp, (2, 1)))

    jj = np.arange(512)
    masks = np.zeros((128, 4 * 512), np.float32)
    for k in range(4):
        masks[:, k * 512:(k + 1) * 512] = (
            jj[None, :] >= (128 * k + np.arange(128))[:, None]).astype(np.float32)
    masks = masks.astype(ml_dtypes.bfloat16)

    swap = np.r_[32:64, 0:32, 96:128, 64:96]
    permM = np.zeros((128, 128), np.float32)
    permM[swap, np.arange(128)] = 1.0
    permM = permM.astype(ml_dtypes.bfloat16)

    x = np.ascontiguousarray(np.asarray(inputs["L0_attn_x"], dtype=np.float32))
    shared = dict(wq_t=wq_t, wk_t=wk_t, wv_t=wv_t, wo_t=wo_t,
                  gate_t=gate_t, up_t=up_t, down_t=down_t,
                  gate_s2=gate_s2, up_s2=up_s2, down_s2=down_s2,
                  cosf=cosf, sinf=sinf, masks=masks, permM=permM,
                  identM=np.eye(128, dtype=np.float32))
    per_core = []
    for c in range(8):
        b, j = c // 4, c % 4
        own = np.r_[512 * j:512 * (j + 1), 512 * (7 - j):512 * (8 - j)]
        xb = x[b]
        d = dict(shared)
        d["x"] = xb
        d["xfm"] = np.ascontiguousarray(xb.T)
        d["xfm_own"] = np.ascontiguousarray(xb.T[:, own])
        d["x_own"] = np.ascontiguousarray(xb[own])
        d["cos_own"] = np.ascontiguousarray(cosf[:, own])
        d["sin_own"] = np.ascontiguousarray(sinf[:, own])
        per_core.append(d)
    return per_core


def build():
    _check_map()
    nc = bass.Bass("TRN2", target_bir_lowering=False)
    t_, v_, c_, g_, y_ = nc.tensor, nc.vector, nc.scalar, nc.gpsimd, nc.sync

    I = {}
    def din(name, shape, dtype=F32):
        I[name] = nc.dram_tensor(name, shape, dtype, kind="ExternalInput").ap()
    din("x", [T, HID]); din("xfm", [HID, T]); din("xfm_own", [HID, 1024])
    din("x_own", [1024, HID]); din("cos_own", [128, 1024]); din("sin_own", [128, 1024])
    din("wq_t", [HID, HID]); din("wk_t", [HID, 512]); din("wv_t", [HID, 512])
    din("wo_t", [HID, HID])
    din("gate_t", [HID, INTER]); din("up_t", [HID, INTER]); din("down_t", [INTER, HID])
    din("gate_s2", [128, 44]); din("up_s2", [128, 44]); din("down_s2", [128, 16])
    din("cosf", [128, T]); din("sinf", [128, T])
    din("masks", [128, 4 * 512], BF16); din("permM", [128, 128], BF16)
    din("identM", [128, 128])
    out = nc.dram_tensor("o", [1024, HID], F32, kind="ExternalOutput").ap()
    xn_dram = nc.dram_tensor("xn_dram", [HID, 1024], BF16, kind="Internal").ap()
    taps = {}
    A = {}
    for name, (off, shape, dt) in SB.items():
        A[name] = nc.alloc_sbuf_tensor_at(name, shape, dt, offset=off).ap()

    psA = nc.alloc_psum_tensor("psA", [128, 2048], F32).ap()
    psB = nc.alloc_psum_tensor("psB", [128, 2048], F32).ap()

    sem = {}
    C = {}
    for n in ["pe", "ac", "ve", "po", "s2", "ss", "sx", "sx1", "scn", "sf", "sf1",
              "sq2", "sr", "sr1", "sw", "sxo", "sod", "sod1", "sout",
              "gk", "gq", "gq1", "ggu", "ggu1", "gdn", "gdn1", "gwo"]:
        sem[n] = nc.semaphore(n).__enter__()
        C[n] = 0

    def inc(instr, s, n=1):
        C[s] += n
        instr.then_inc(sem[s], n)
        return C[s]

    def W(eng, s, val=None):
        if val is None:
            val = C[s]
        if val > 0:
            eng.wait_ge(sem[s], val)

    def dma(q, s, out, in_):
        W(q, s)
        return inc(q.dma_start(out=out, in_=in_), s, 16)

    cnt = {}
    body = nc.body("main", valid_engines=list(mybir.ALL_ENGINES))
    body.__enter__()

    v_.memset(A["c_m8"], -8.0)
    v_.memset(A["c_eps"], 1e-5)
    v_.memset(A["ones64"], 1.0)
    v_.memset(A["ones128"], 1.0)
    vview = A["V_sb"].rearrange("p (s k d) -> p s k d", s=32, k=8)
    v_.memset(A["kbfA"], 0.0)
    v_.memset(A["kbfQ"], 0.0)
    ve_init = inc(v_.memset(vview[:, :, :, 64:65], 1.0), "ve")

    for nm, src in [("identM", "identM"), ("permM", "permM"),
                    ("gateS", "gate_s2"), ("upS", "up_s2"), ("downS", "down_s2")]:
        dma(y_, "ss", A[nm], I[src])
    sd_small = C["ss"]

    wkvv = A["wkv"].rearrange("p (h w) -> p h w", h=16)
    dma(g_, "gk", wkvv[:, :, 0:512], I["wk_t"].rearrange("(h p) o -> p h o", p=128))
    gw_wkv = dma(g_, "gk", wkvv[:, :, 512:1024],
                 I["wv_t"].rearrange("(h p) o -> p h o", p=128))

    def ln_stats(xtile, ti, rows_g, rows_n):
        sc = A["scol"][:, 8 * (ti % 2): 8 * (ti % 2) + 8]
        W(c_, "sx")
        W(c_, "sx1")
        W(c_, "ve")
        vv = inc(c_.activation(A["sq_j"], xtile, AF.Square, accum_out=sc[:, 1:2]), "ac")
        W(c_, "ac", vv)
        ac_sq = inc(c_.activation(A["sq_j"], xtile, AF.Identity, accum_out=sc[:, 0:1]),
                    "ac")
        W(v_, "ac", ac_sq)
        vv = inc(v_.tensor_scalar_mul(sc[:, 2:3], sc[:, 0:1], 1.0 / 2048.0), "ve")
        W(v_, "ve", vv)
        vv = inc(v_.tensor_mul(sc[:, 3:4], sc[:, 2:3], sc[:, 2:3]), "ve")
        W(v_, "ve", vv)
        ve_var = inc(v_.scalar_tensor_tensor(sc[:, 4:5], sc[:, 1:2], 1.0 / 2048.0,
                                             sc[:, 3:4], ALU.mult, ALU.subtract), "ve")
        W(c_, "ve", ve_var)
        vv = inc(c_.activation(sc[:, 5:6], sc[:, 4:5], AF.Ln, bias=A["c_eps"]), "ac")
        W(c_, "ac", vv)
        ac_rstd = inc(c_.activation(sc[:, 6:7], sc[:, 5:6], AF.Exp, scale=-0.5), "ac")
        W(v_, "ac", ac_rstd)
        ve_nmr = inc(v_.scalar_tensor_tensor(sc[:, 7:8], sc[:, 2:3], -1.0, sc[:, 6:7],
                                             ALU.mult, ALU.mult), "ve")
        W(t_, "ve", ve_nmr)
        W(t_, "ss", sd_small)
        t_.transpose(psA[0:1, 0:128], sc[:, 6:7], A["identM"])
        pe_tr = inc(t_.transpose(psA[0:1, 512:640], sc[:, 7:8], A["identM"]), "pe")
        W(v_, "pe", pe_tr)
        v_.tensor_copy(rows_g[0:1, ti * 128:(ti + 1) * 128], psA[0:1, 0:128])
        ve_row = inc(v_.tensor_copy(rows_n[0:1, ti * 128:(ti + 1) * 128],
                                    psA[0:1, 512:640]), "ve")
        return ve_row, ac_sq

    for ti in range(8):
        buf = A["x_t"][:, (ti % 2) * 2048:(ti % 2) * 2048 + 2048]
        if ti >= 2:
            W(y_, "ve", cnt[("row", "own", ti - 2)])
            W(y_, "ac", cnt[("sq", "own", ti - 2)])
        dma(y_, "sx" if ti % 2 == 0 else "sx1", buf, I["x_own"][ti * 128:(ti + 1) * 128, :])
        r, s = ln_stats(buf, ti, A["rstd_own"], A["nmr_own"])
        cnt[("row", "own", ti)] = r
        cnt[("sq", "own", ti)] = s
    ve_rows_own = C["ve"]

    for tg in range(8):
        ve_rows = 0
        for ti in range(4):
            buf = A["x_t"][:, (ti % 2) * 2048:(ti % 2) * 2048 + 2048]
            prev = cnt.get(("row", tg, ti - 2), cnt.get(("row", tg - 1, ti + 2),
                           cnt.get(("row", "own", ti + 6), 0)))
            prevs = cnt.get(("sq", tg, ti - 2), cnt.get(("sq", tg - 1, ti + 2),
                            cnt.get(("sq", "own", ti + 6), 0)))
            W(y_, "ve", prev)
            W(y_, "ac", prevs)
            dma(y_, "sx" if ti % 2 == 0 else "sx1", buf,
                I["x"][tg * 512 + ti * 128: tg * 512 + (ti + 1) * 128, :])
            r, s = ln_stats(buf, ti, A["rows_g"], A["rows_n"])
            cnt[("row", tg, ti)] = r
            cnt[("sq", tg, ti)] = s
            ve_rows = r
        W(y_, "ve", cnt.get(("kadd", tg - 1, 3), 0))
        dma(y_, "scn", A["csn"][:, 0:512], I["cosf"][:, tg * 512:(tg + 1) * 512])
        dma(y_, "scn", A["csn"][:, 512:1024], I["sinf"][:, tg * 512:(tg + 1) * 512])
        sd_csn = C["scn"]
        W(t_, "ve", ve_rows)
        t_.matmul(psA[:, 1024:1536], A["ones128"], A["rows_g"], start=True, stop=True)
        pe_rep = inc(t_.matmul(psA[:, 1536:2048], A["ones128"], A["rows_n"],
                               start=True, stop=True), "pe")
        W(v_, "pe", pe_rep)
        W(v_, "po", cnt.get(("app", tg - 1, NHC - 1), 0))
        v_.tensor_copy(A["reps"][:, 0:512], psA[:, 1024:1536])
        ve_rep = inc(v_.tensor_copy(A["reps"][:, 512:1024], psA[:, 1536:2048]), "ve")
        xnr = A["xn_rot"] if tg % 2 == 0 else A["xn_rot2"]
        xfmv = I["xfm"].rearrange("(c p) t -> p c t", p=128)
        for hp in range(8):
            fb = A["xfm_t"][:, (hp % 2) * 1024:(hp % 2) * 1024 + 1024]
            prev = cnt.get(("appm", tg, 2 * hp - 3), cnt.get(("appm", tg - 1, 12 + 2 * hp + 1), 0))
            W(y_, "ve", prev)
            dma(y_, "sf" if hp % 2 == 0 else "sf1",
                fb.rearrange("p (c t) -> p c t", c=2),
                xfmv[:, 2 * hp:2 * hp + 2, tg * 512:(tg + 1) * 512])
            for sub in range(2):
                h = 2 * hp + sub
                W(v_, "sf" if hp % 2 == 0 else "sf1")
                if h == 0:
                    W(v_, "ve", ve_rep)
                ab = A["appt"][:, (h % 2) * 512:(h % 2) * 512 + 512]
                if h >= 2:
                    W(v_, "po", cnt.get(("app", tg, h - 2), 0))
                cnt[("appm", tg, h)] = inc(
                    v_.tensor_mul(ab, fb[:, sub * 512:(sub + 1) * 512],
                                  A["reps"][:, 0:512]), "ve")
                W(g_, "ve", cnt[("appm", tg, h)])
                if h == 0:
                    W(g_, "pe", cnt.get(("proj", tg - 2), 0))
                cnt[("app", tg, h)] = inc(
                    g_.tensor_add(xnr[:, h * 512:(h + 1) * 512], ab,
                                  A["reps"][:, 512:1024]), "po")
        W(t_, "gk", gw_wkv)
        W(t_, "po", cnt[("app", tg, NHC - 1)])
        W(t_, "ve", cnt.get(("kt1", tg - 1, 3), 0))
        for h in range(NHC):
            for ck in range(4):
                i = t_.matmul(psB[:, ck * 512:(ck + 1) * 512],
                              A["wkv"][:, h * 1024 + ck * 128: h * 1024 + (ck + 1) * 128],
                              xnr[:, h * 512:(h + 1) * 512],
                              start=(h == 0), stop=(h == NHC - 1))
        pe_K = inc(i, "pe")
        for ck in range(4):
            kb = A["kbfA"][:, (ck % 2) * 512:(ck % 2) * 512 + 512]
            W(v_, "pe", pe_K)
            W(v_, "pe", cnt.get(("ksw", tg, ck - 2), 0))
            ve_kb = inc(v_.tensor_copy(kb, psB[:, ck * 512:(ck + 1) * 512]), "ve")
            W(t_, "ve", ve_kb)
            W(t_, "ve", cnt.get(("kadd", tg, ck - 2), 0))
            cnt[("ksw", tg, ck)] = inc(
                t_.matmul(psA[:, (ck % 2) * 512:(ck % 2) * 512 + 512], A["permM"], kb,
                          start=True, stop=True), "pe")
            W(v_, "scn", sd_csn)
            W(v_, "pe", cnt[("ksw", tg, ck)])
            tb = A["t1A"][:, (ck % 2) * 512:(ck % 2) * 512 + 512]
            cnt[("kt1", tg, ck)] = inc(
                v_.tensor_mul(tb, psB[:, ck * 512:(ck + 1) * 512], A["csn"][:, 0:512]), "ve")
            vv = inc(v_.tensor_mul(psA[:, (2 + ck % 2) * 512:(2 + ck % 2) * 512 + 512],
                                   psA[:, (ck % 2) * 512:(ck % 2) * 512 + 512],
                                   A["csn"][:, 512:1024]), "ve")
            W(v_, "ve", vv)
            cnt[("kadd", tg, ck)] = inc(
                v_.tensor_add(A["K_sb"][:, ck * 4096 + tg * 512: ck * 4096 + (tg + 1) * 512],
                              tb, psA[:, (2 + ck % 2) * 512:(2 + ck % 2) * 512 + 512]), "ve")
        W(t_, "ve", cnt[("kadd", tg, 3)])
        W(t_, "ve", cnt.get(("vev", tg - 1, 3), 0))
        for h in range(NHC):
            for tv in range(4):
                i = t_.matmul(psA[:, tv * 512:(tv + 1) * 512],
                              xnr[:, h * 512 + tv * 128: h * 512 + (tv + 1) * 128],
                              A["wkv"][:, h * 1024 + 512: h * 1024 + 1024],
                              start=(h == 0), stop=(h == NHC - 1))
        pe_V = inc(i, "pe")
        cnt[("proj", tg)] = C["pe"]
        W(v_, "pe", pe_V)
        if tg == 0:
            W(v_, "ve", ve_init)
        for tv in range(4):
            dst = vview[:, tg * 4 + tv, :, 0:64]
            cnt[("vev", tg, tv)] = inc(
                v_.tensor_copy(dst, psA[:, tv * 512:(tv + 1) * 512]
                               .rearrange("p (k d) -> p k d", k=8)), "ve")

    ve_A_end = C["ve"]

    W(t_, "ve", ve_A_end)
    for half in range(2):
        t_.matmul(psA[:, half * 512:(half + 1) * 512], A["ones128"],
                  A["rstd_own"][0:1, half * 512:(half + 1) * 512], start=True, stop=True)
        i = t_.matmul(psA[:, 1024 + half * 512: 1024 + (half + 1) * 512], A["ones128"],
                      A["nmr_own"][0:1, half * 512:(half + 1) * 512], start=True, stop=True)
    pe_orep = inc(i, "pe")
    W(v_, "pe", pe_orep)
    v_.tensor_copy(A["reps_o"][:, 0:1024], psA[:, 0:1024])
    ve_orep = inc(v_.tensor_copy(A["reps_o"][:, 1024:2048], psA[:, 1024:2048]), "ve")

    for h in range(NHC):
        fb = A["xfm_ow"][:, (h % 2) * 1024:(h % 2) * 1024 + 1024]
        if h < 2:
            W(y_, "ve", cnt[("row", 7, 3)])
            W(y_, "ac", cnt[("sq", 7, 3)])
        W(y_, "ve", cnt.get(("oxn", h - 2), 0))
        dma(y_, "sf" if h % 2 == 0 else "sf1", fb, I["xfm_own"][h * 128:(h + 1) * 128, :])
        W(v_, "sf" if h % 2 == 0 else "sf1")
        if h == 0:
            W(v_, "ve", ve_orep)
        W(v_, "sw", cnt.get(("oxnd", h - 1), 0))
        vv = inc(v_.tensor_mul(A["t1Q"], fb, A["reps_o"][:, 0:1024]), "ve")
        W(v_, "ve", vv)
        cnt[("oxn", h)] = inc(v_.tensor_add(A["xnb_Q"][:, 0:1024], A["t1Q"],
                                            A["reps_o"][:, 1024:2048]), "ve")
        W(y_, "ve", cnt[("oxn", h)])
        dma(y_, "sw", xn_dram[h * 128:(h + 1) * 128, :], A["xnb_Q"][:, 0:1024])
        cnt[("oxnd", h)] = C["sw"]
    ou_xn = C["sw"]

    W(y_, "ve", cnt[("kadd", 7, 3)])
    dma(y_, "sq2", A["cosownQ"], I["cos_own"])
    dma(y_, "sq2", A["sinownQ"], I["sin_own"])
    sd_csown = C["sq2"]
    W(y_, "ve", cnt[("oxn", NHC - 1)])
    dma(y_, "sq2", A["masks"], I["masks"])
    sd_masks = C["sq2"]

    wqv = A["wq_st"].rearrange("p (b h w) -> p b h w", b=2, h=16)
    for op8 in range(8):
        if op8 < 2:
            W(g_, "pe", cnt[("proj", 7)])
        W(g_, "pe", cnt.get(("qmm", op8 - 2, NHC - 1), 0))
        gw_wq = dma(g_, "gq" if op8 % 2 == 0 else "gq1", wqv[:, op8 % 2],
                    I["wq_t"].rearrange("(h p) o -> p h o", p=128)[:, :, op8 * 256:(op8 + 1) * 256])
        for h in range(NHC):
            xb = A["xnb_Q"][:, (1 + h % 2) * 1024:(1 + h % 2) * 1024 + 1024]
            prev = cnt.get(("qmm", op8, h - 2), cnt.get(("qmm", op8 - 1, 14 + h), 0))
            W(y_, "pe", prev)
            W(y_, "sw", 16 * (h + 1))
            dma(y_, "sr" if h % 2 == 0 else "sr1", xb, xn_dram[h * 128:(h + 1) * 128, :])
            W(t_, "sr" if h % 2 == 0 else "sr1")
            if h == 0:
                W(t_, "gq" if op8 % 2 == 0 else "gq1", gw_wq)
                W(t_, "ve", cnt.get(("qrope", op8 - 1), ve_orep))
            for oc in range(2):
                for th in range(2):
                    i = t_.matmul(psA[:, (oc * 2 + th) * 512:(oc * 2 + th + 1) * 512],
                                  wqv[:, op8 % 2, h, oc * 128:(oc + 1) * 128],
                                  xb[:, th * 512:(th + 1) * 512],
                                  start=(h == 0), stop=(h == NHC - 1))
            cnt[("qmm", op8, h)] = inc(i, "pe")
        pe_q = cnt[("qmm", op8, NHC - 1)]
        for oc in range(2):
            ch = op8 * 2 + oc
            W(v_, "pe", pe_q)
            W(v_, "pe", cnt.get(("qsw", ch - 1), 0))
            ve_kb = inc(v_.tensor_copy(A["kbfQ"], psA[:, oc * 1024:(oc + 1) * 1024]), "ve")
            W(t_, "ve", ve_kb)
            if ch == 0:
                W(t_, "ve", ve_A_end)
            else:
                W(t_, "ve", cnt.get(("qadd", ch - 1), 0))
            t_.matmul(psB[:, 0:512], A["permM"], A["kbfQ"][:, 0:512], start=True, stop=True)
            cnt[("qsw", ch)] = inc(
                t_.matmul(psB[:, 512:1024], A["permM"], A["kbfQ"][:, 512:1024],
                          start=True, stop=True), "pe")
            W(v_, "sq2", sd_csown)
            W(v_, "pe", cnt[("qsw", ch)])
            vv = inc(v_.tensor_mul(A["t1Q"], psA[:, oc * 1024:(oc + 1) * 1024],
                                   A["cosownQ"]), "ve")
            vv = inc(v_.tensor_mul(psB[:, 1024:2048], psB[:, 0:1024], A["sinownQ"]), "ve")
            W(v_, "ve", vv)
            cnt[("qadd", ch)] = inc(
                v_.tensor_add(A["Q_sb"][:, ch * 1024:(ch + 1) * 1024], A["t1Q"],
                              psB[:, 1024:2048]), "ve")
        cnt[("qrope", op8)] = C["ve"]

    ve_Q_end = C["ve"]
    pe_Q_end = C["pe"]
    ac_Q_end = C["ac"]
    pid_t = t_.compute_val(t_.partition_id() % 4)
    pid_c = c_.compute_val(c_.partition_id() % 4)
    pid_v = v_.compute_val(v_.partition_id() % 4)
    body.__exit__(None, None, None)

    snap = dict(C)
    final_counts = None
    for var in nc.Switch(engines=[t_, c_, v_], index=[pid_t, pid_c, pid_v], n=4):
        C.clear(); C.update(snap)
        j = var
        nodd = 0
        cnt_s2 = [0]
        firstL = [True]
        vmask_first = True
        exp_cnt = {}
        for gi, g in enumerate([j, 7 - j]):
            nsb = 4 * (g + 1)
            for ch in range(16):
                m = ch // 4
                av_free = C["ve"]
                pe_av = 0
                qk_cnt = {}

                def emit_qk(sb):
                    par = sb % 2
                    if (ch, sb - 2) in exp_cnt:
                        W(t_, "ac", exp_cnt[(ch, sb - 2)])
                    else:
                        W(t_, "ac")
                    for hh in range(2):
                        b0 = 64 * hh
                        i = t_.matmul(
                            psA[:, par * 1024 + hh * 512: par * 1024 + (hh + 1) * 512],
                            A["K_sb"][b0:b0 + 64,
                                      m * 4096 + sb * 128: m * 4096 + (sb + 1) * 128],
                            A["Q_sb"][b0:b0 + 64,
                                      ch * 1024 + gi * 512: ch * 1024 + (gi + 1) * 512],
                            start=True, stop=True)
                    qk_cnt[sb] = inc(i, "pe")

                if firstL[0]:
                    W(t_, "ve", ve_Q_end)
                    W(t_, "ac", ac_Q_end)
                    firstL[0] = False
                emit_qk(0)
                for sb in range(nsb):
                    par = sb % 2
                    if sb + 1 < nsb:
                        emit_qk(sb + 1)
                    W(c_, "pe", qk_cnt[sb])
                    eb = A["Ebuf"][:, (sb % 2) * 1024:(sb % 2) * 1024 + 1024]
                    ac_exp = inc(c_.activation(eb, psA[:, par * 1024:(par + 1) * 1024],
                                               AF.Exp, bias=A["c_m8"], scale=0.125), "ac")
                    exp_cnt[(ch, sb)] = ac_exp
                    diag = sb >= 4 * g
                    if diag:
                        k = sb - 4 * g
                        if vmask_first:
                            W(v_, "sq2", sd_masks)
                            vmask_first = False
                        W(v_, "ac", ac_exp)
                        W(v_, "pe")
                        for hh in range(2):
                            i = v_.tensor_mul(A["Emask"][:, hh * 512:(hh + 1) * 512],
                                              eb[:, hh * 512:(hh + 1) * 512],
                                              A["masks"][:, k * 512:(k + 1) * 512])
                        ve_mask = inc(i, "ve")
                        W(t_, "ve", ve_mask)
                        src = A["Emask"]
                    else:
                        W(t_, "ac", ac_exp)
                        src = eb
                    if sb == 0:
                        W(t_, "ve", av_free)
                        W(t_, "s2", cnt_s2[0])
                    for hh in range(2):
                        kv = 2 * m + hh
                        i = t_.matmul(
                            psB[0:65, hh * 512:(hh + 1) * 512],
                            A["V_sb"].rearrange("p (s w) -> p s w", s=32)
                            [:, sb, kv * 65:(kv + 1) * 65],
                            src[:, hh * 512:(hh + 1) * 512],
                            start=(sb == 0), stop=(sb == nsb - 1))
                    pe_av = inc(i, "pe")
                for hh in range(2):
                    head = 8 * (ch // 4) + (ch % 4) + 4 * hh
                    W(v_, "pe")
                    ve_rd = inc(v_.reciprocal(A["rdbuf"],
                                              psB[64:65, hh * 512:(hh + 1) * 512]), "ve")
                    W(t_, "ve", ve_rd)
                    i = t_.matmul(psA[0:64, 1024 + hh * 512: 1024 + hh * 512 + 512]
                                  if False else
                                  psB[0:64, 1024 + hh * 512: 1024 + hh * 512 + 512],
                                  A["ones64"], A["rdbuf"], start=True, stop=True)
                    pe_bc = inc(i, "pe")
                    bcb = A["bcbuf0"] if hh == 0 else A["bcbuf1"]
                    W(v_, "pe", pe_bc)
                    ve_bc = inc(v_.tensor_copy(bcb, psB[0:64, 1024 + hh * 512:
                                                        1024 + hh * 512 + 512]), "ve")
                    W(v_, "ve", ve_bc)
                    dst_cols = slice((head // 2) * 1024 + gi * 512,
                                     (head // 2) * 1024 + (gi + 1) * 512)
                    if head % 2 == 0:
                        i = v_.tensor_mul(A["A_sb"][0:64, dst_cols],
                                          psB[0:64, hh * 512:(hh + 1) * 512], bcb)
                        inc(i, "ve")
                    else:
                        par2 = nodd % 2
                        W(v_, "s2", cnt_s2[0] - 16 if nodd >= 2 else 0)
                        i = v_.tensor_mul(A["antmp"][:, par2 * 512:(par2 + 1) * 512],
                                          psB[0:64, hh * 512:(hh + 1) * 512], bcb)
                        ve_an = inc(i, "ve")
                        W(c_, "ve", ve_an)
                        W(c_, "s2")
                        inc(c_.dma_start(out=A["A_sb"][64:128, dst_cols],
                                         in_=A["antmp"][:, par2 * 512:(par2 + 1) * 512]),
                            "s2", 16)
                        cnt_s2[0] = C["s2"]
                        nodd += 1
        if final_counts is None:
            final_counts = dict(C)
        else:
            assert final_counts == C, "switch variant sem divergence"
    C.clear(); C.update(final_counts)

    body2 = nc.body("tail", valid_engines=list(mybir.ALL_ENGINES))
    body2.__enter__()
    ve_B_end = C["ve"]
    pe_B_end = C["pe"]
    ac_B_end = C["ac"]
    s2_B_end = C["s2"]

    for h in range(NHC):
        if h == 0:
            W(y_, "pe", pe_Q_end)
            W(y_, "ve", ve_Q_end)
        dma(y_, "sxo", A["xn_own"][:, h * 1024:(h + 1) * 1024],
            xn_dram[h * 128:(h + 1) * 128, :])
    sd_xno = C["sxo"]

    guv_g = A["gu_g"].rearrange("p (b h w) -> p b h w", b=2, h=16)
    guv_u = A["gu_u"].rearrange("p (b h w) -> p b h w", b=2, h=16)
    gatev = I["gate_t"].rearrange("(h p) i -> p h i", p=128)
    upv = I["up_t"].rearrange("(h p) i -> p h i", p=128)
    for ic in range(NIC):
        if ic < 2:
            W(g_, "ve", ve_B_end)
        W(g_, "pe", cnt.get(("fmm", ic - 2, NHC - 1), 0))
        gs_ = "ggu" if ic % 2 == 0 else "ggu1"
        dma(g_, gs_, guv_g[:, ic % 2], gatev[:, :, ic * 128:(ic + 1) * 128])
        gw_gu = dma(g_, gs_, guv_u[:, ic % 2], upv[:, :, ic * 128:(ic + 1) * 128])
        ps = psA if ic % 2 == 0 else psB
        for h in range(NHC):
            if h == 0:
                W(t_, "sxo", sd_xno)
                W(t_, "ggu" if ic % 2 == 0 else "ggu1", gw_gu)
                if ic < 2:
                    W(t_, "ve", ve_B_end)
                    W(t_, "ac", ac_B_end)
                else:
                    W(t_, "ac", cnt.get(("fsil", ic - 2), 0))
                    W(t_, "ve", cnt.get(("fH", ic - 2), 0))
            for th in range(2):
                t_.matmul(ps[:, th * 512:(th + 1) * 512],
                          guv_g[:, ic % 2, h, :],
                          A["xn_own"][:, h * 1024 + th * 512: h * 1024 + (th + 1) * 512],
                          start=(h == 0), stop=(h == NHC - 1))
            for th in range(2):
                i = t_.matmul(ps[:, 1024 + th * 512: 1024 + (th + 1) * 512],
                              guv_u[:, ic % 2, h, :],
                              A["xn_own"][:, h * 1024 + th * 512: h * 1024 + (th + 1) * 512],
                              start=(h == 0), stop=(h == NHC - 1))
        cnt[("fmm", ic, NHC - 1)] = inc(i, "pe")
        W(c_, "pe", cnt[("fmm", ic, NHC - 1)])
        W(c_, "ve", cnt.get(("fH", ic - 1), 0))
        cnt[("fsil", ic)] = inc(
            c_.activation(A["gtC"], ps[:, 0:1024], AF.Silu,
                          scale=A["gateS"][:, ic:ic + 1]), "ac")
        W(v_, "ac", cnt[("fsil", ic)])
        if ic == 0:
            W(v_, "s2", s2_B_end)
        cnt[("fH", ic)] = inc(
            v_.scalar_tensor_tensor(A["H_sb"][:, ic * 1024:(ic + 1) * 1024],
                                    ps[:, 1024:2048], A["upS"][:, ic:ic + 1], A["gtC"],
                                    ALU.mult, ALU.mult), "ve")
    ve_C_end = C["ve"]
    pe_C_end = C["pe"]

    dnv = A["dn_st"].rearrange("p (b i w) -> p b i w", b=2, i=22)
    wov = A["wo_st"].rearrange("p (b h w) -> p b h w", b=2, h=16)
    downv = I["down_t"].rearrange("(i p) o -> p i o", p=128)
    wodv = I["wo_t"].rearrange("(h p) o -> p h o", p=128)
    for ob in range(8):
        for half in range(2):
            if ob == 0:
                W(g_, "pe", pe_C_end)
            W(g_, "pe", cnt.get(("dmm", ob - 1, half), 0))
            cnt[("gwdn", ob, half)] = dma(
                g_, "gdn" if half == 0 else "gdn1", dnv[:, half],
                downv[:, half * 22:(half + 1) * 22, ob * 256:(ob + 1) * 256])
        if ob == 0:
            W(g_, "pe", pe_C_end)
        W(g_, "pe", cnt.get(("wmm", ob - 2), 0))
        cnt[("gwwo", ob)] = dma(g_, "gwo", wov[:, ob % 2],
                                wodv[:, :, ob * 256:(ob + 1) * 256])
        for i44 in range(NIC):
            half, idx = (0, i44) if i44 < 22 else (1, i44 - 22)
            if i44 == 0:
                W(t_, "ve", ve_C_end)
                W(t_, "gdn", cnt[("gwdn", ob, 0)])
                W(t_, "ac", cnt.get(("dg", ob - 1, 1), 0))
                W(t_, "ve", cnt.get(("dadd", ob - 1, 1, 7), 0))
            if i44 == 22:
                W(t_, "gdn1", cnt[("gwdn", ob, 1)])
            for oc in range(2):
                for th in range(2):
                    i = t_.matmul(psA[:, (oc * 2 + th) * 512:(oc * 2 + th + 1) * 512],
                                  dnv[:, half, idx, oc * 128:(oc + 1) * 128],
                                  A["H_sb"][:, i44 * 1024 + th * 512:
                                            i44 * 1024 + (th + 1) * 512],
                                  start=(i44 == 0), stop=(i44 == NIC - 1))
            if i44 == 21:
                cnt[("dmm", ob, 0)] = inc(i, "pe")
        cnt[("dmm", ob, 1)] = pe_dn = inc(i, "pe")
        for h in range(NHC):
            if h == 0:
                W(t_, "gwo", cnt[("gwwo", ob)])
                W(t_, "ve", ve_B_end)
                W(t_, "s2", s2_B_end)
                W(t_, "ve", cnt.get(("dsg", ob - 1, 1), 0))
            for oc in range(2):
                for th in range(2):
                    i = t_.matmul(psB[:, (oc * 2 + th) * 512:(oc * 2 + th + 1) * 512],
                                  wov[:, ob % 2, h, oc * 128:(oc + 1) * 128],
                                  A["A_sb"][:, h * 1024 + th * 512:
                                            h * 1024 + (th + 1) * 512],
                                  start=(h == 0), stop=(h == NHC - 1))
        cnt[("wmm", ob)] = pe_wo = inc(i, "pe")
        for oc in range(2):
            occ = 2 * ob + oc
            W(c_, "pe", pe_dn)
            W(c_, "ve", cnt.get(("dsg", ob, oc - 1), cnt.get(("dsg", ob - 1, 1), 0)))
            cnt[("dg", ob, oc)] = inc(
                c_.activation(A["g_D"], psA[:, oc * 1024:(oc + 1) * 1024], AF.Silu,
                              scale=A["downS"][:, occ:occ + 1]), "ac")
            W(v_, "pe", pe_wo)
            W(v_, "ac", cnt[("dg", ob, oc)])
            cnt[("dsg", ob, oc)] = inc(
                v_.tensor_mul(A["sg_D"][:, oc * 1024:(oc + 1) * 1024],
                              psB[:, oc * 1024:(oc + 1) * 1024], A["g_D"]), "ve")
        for tt in range(8):
            xo = A["xo_D"][:, (tt % 2) * 256:(tt % 2) * 256 + 256]
            if (ob, tt) == (0, 0):
                W(y_, "pe", pe_C_end)
            W(y_, "ve", cnt.get(("dadd", ob, 1, tt - 2),
                                cnt.get(("dadd", ob - 1, 1, 6 + tt), 0)))
            sods = "sod" if tt % 2 == 0 else "sod1"
            dma(y_, sods, xo, I["x_own"][tt * 128:(tt + 1) * 128,
                                         ob * 256:(ob + 1) * 256])
            sd_xo = C[sods]
            for oc in range(2):
                occ = 2 * ob + oc
                W(t_, "ve", cnt[("dsg", ob, 1)])
                W(t_, "ac", cnt[("dg", ob, 1)])
                W(t_, "ve", cnt.get(("dadd", ob, oc, tt - 2),
                                    cnt.get(("dadd", ob - 1, oc, 6 + tt), 0)))
                bank = 2 * (tt % 2) + oc
                i = t_.transpose(psA[:, bank * 512: bank * 512 + 128],
                                 A["sg_D"][:, oc * 1024 + tt * 128:
                                           oc * 1024 + (tt + 1) * 128], A["identM"])
                pe_tr = inc(i, "pe")
                W(v_, "pe", pe_tr)
                W(v_, sods, sd_xo)
                W(v_, "sout", cnt.get(("dout", ob, tt - 4),
                                      cnt.get(("dout", ob - 1, 4 + tt), 0)))
                cnt[("dadd", ob, oc, tt)] = inc(
                    v_.tensor_add(A["out_D"][:, (tt % 4) * 256 + oc * 128:
                                             (tt % 4) * 256 + oc * 128 + 128],
                                  psA[:, bank * 512: bank * 512 + 128],
                                  xo[:, oc * 128:(oc + 1) * 128]), "ve")
            W(y_, "ve", cnt[("dadd", ob, 1, tt)])
            dma(y_, "sout", out[tt * 128:(tt + 1) * 128, ob * 256:(ob + 1) * 256],
                A["out_D"][:, (tt % 4) * 256:(tt % 4) * 256 + 256])
            cnt[("dout", ob, tt)] = C["sout"]

    W(y_, "sout", C["sout"])
    body2.__exit__(None, None, None)
    return nc


def kernel(**inputs):
    per_core = host_layout(inputs)
    nc = build()
    res = run_bass_kernel_spmd(nc, per_core, core_ids=list(range(8)))
    outp = np.zeros((2, 4096, 2048), np.float32)
    for c in range(8):
        b, j = c // 4, c % 4
        r = np.asarray(res.results[c]["o"], np.float32)
        outp[b, 512 * j:512 * (j + 1)] = r[0:512]
        outp[b, 512 * (7 - j):512 * (8 - j)] = r[512:1024]
    return outp


if __name__ == "__main__":
    pass


# version 07
# speedup vs baseline: 1.3062x; 1.0197x over previous
import numpy as np
import ml_dtypes
import concourse.bass as bass
import concourse.mybir as mybir
from concourse.bass_utils import run_bass_kernel_spmd

F32 = mybir.dt.float32
BF16 = mybir.dt.bfloat16
AF = mybir.ActivationFunctionType
ALU = mybir.AluOpType
AX = mybir.AxisListType

T, HID, INTER = 4096, 2048, 5632
NHC, NIC = 16, 44
KB = 1024

SB = {}
SB_BASE = 13 * KB
def _sb(name, off_kb, shape, dtype):
    assert name not in SB
    off = int(off_kb * KB) + SB_BASE
    off = (off + 31) // 32 * 32
    SB[name] = (off, shape, dtype)

_sb("c_m8",     4.000, [128, 1], F32)
_sb("c_eps",    4.0625, [128, 1], F32)
_sb("ones64",   4.125, [1, 64], F32)
_sb("ones128",  4.375, [1, 128], F32)
_sb("gateS",    4.875, [128, 44], F32)
_sb("upS",      5.0625, [128, 44], F32)
_sb("downS",    5.25, [128, 16], F32)
_sb("permM",    5.3125, [128, 128], BF16)
_sb("identM",   5.5625, [128, 128], F32)
_sb("rstd_own", 6.25, [1, 1024], F32)
_sb("nmr_own", 10.25, [1, 1024], F32)
_sb("masks",    14.25, [128, 4 * 512], BF16)
_sb("gu_g",   110.0, [128, 2 * 16 * 128], BF16)
_sb("gu_u",   118.0, [128, 2 * 16 * 128], BF16)
_sb("K_sb",    22.0, [128, 4 * 4096], BF16)
_sb("V_sb",    54.0, [128, 32 * 520], BF16)
_sb("H_sb",    22.0, [128, 44 * 1024], BF16)
_sb("Emask",   88.0, [128, 2048], BF16)
_sb("gtC",    132.5, [128, 1024], F32)
_sb("Ebuf",    92.5, [128, 2048], BF16)
_sb("nd_sb",   96.5, [65, 1024], F32)
_sb("rdbuf",  101.0, [1, 512], F32)
_sb("bcbuf0", 103.5, [64, 512], F32)
_sb("bcbuf1", 105.5, [64, 512], F32)
_sb("antmp",  107.5, [64, 2 * 512], BF16)
_sb("xn_rot", 110.0, [128, 16 * 512], BF16)
_sb("xn_rot2", 92.5, [128, 16 * 512], BF16)
_sb("wkv",    126.0, [128, 16 * 1024], BF16)
_sb("Q_sb",   110.0, [128, 16 * 1024], BF16)
_sb("dn_st",  110.0, [128, 2 * 22 * 256], BF16)
_sb("csn",    158.0, [128, 2 * 512], F32)
_sb("xfm_t",  162.0, [128, 2 * 1024], F32)
_sb("kbfA",   170.0, [128, 2 * 512], BF16)
_sb("t1A",    172.0, [128, 2 * 512], F32)
_sb("appt",   176.0, [128, 2 * 512], F32)
_sb("wq_st",  142.5, [128, 2 * 16 * 256], BF16)
_sb("cosownQ", 88.0, [128, 1024], F32)
_sb("sinownQ", 92.5, [128, 1024], F32)
_sb("kbfQ",    96.5, [128, 1024], BF16)
_sb("t1Q",    158.6, [128, 1024], F32)
_sb("xn_own", 142.5, [128, 16 * 1024], BF16)
_sb("wo_st",  142.5, [128, 2 * 16 * 256], BF16)
_sb("g_D",    158.5, [128, 1024], F32)
_sb("sg_D",   162.5, [128, 2 * 1024], F32)
_sb("xo_D",   170.5, [128, 2 * 256], F32)
_sb("out_D",  172.5, [128, 4 * 256], F32)
_sb("x_t",    180.5, [128, 2 * 2048], F32)
_sb("sq_j",   196.5, [128, 2048], BF16)
_sb("sq_j2",   88.0, [128, 2048], BF16)
_sb("parts",  200.6, [128, 16], F32)
_sb("scol",   200.7, [128, 16], F32)
_sb("rows_g", 200.8, [1, 512], F32)
_sb("rows_n", 202.8, [1, 512], F32)
_sb("reps",   204.8, [128, 2 * 512], F32)
_sb("xnb_Q",  176.5, [128, 1024], BF16)
_sb("xnp_Q",   98.5, [128, 2 * 2048], BF16)
_sb("xfm_ow", 182.5, [128, 2 * 1024], F32)
_sb("reps_o", 190.5, [128, 2 * 1024], F32)
_sb("A_sb",   174.5, [128, 16 * 1024], BF16)
SB["out_D"] = (int(170.5 * KB) + SB_BASE, [128, 4 * 256], F32)
SB["xo_D"] = (int(136.5 * KB) + SB_BASE, [128, 2 * 256], F32)
SB_LIMIT = 229 * KB


def _esz(dt):
    return 2 if dt == BF16 else 4


def _check_map():
    for name, (off, shape, dt) in SB.items():
        n = 1
        for s in shape[1:]:
            n *= s
        assert off + n * _esz(dt) <= SB_LIMIT, f"{name} out of sbuf"
    LIFE = {
        "c_m8": "WAQBCD", "c_eps": "WAQBCD", "ones64": "WAQBCD", "ones128": "WAQBCD",
        "gateS": "WAQBCD", "upS": "WAQBCD", "downS": "WAQBCD", "permM": "WAQBCD",
        "identM": "WAQBCD",
        "rstd_own": "AQ", "nmr_own": "AQ", "masks": "B", "gu_g": "C", "gu_u": "C",
        "K_sb": "AQB", "V_sb": "AQB", "H_sb": "CD",
        "Emask": "B", "gtC": "C", "Ebuf": "B", "nd_sb": "B", "rdbuf": "B", "bcbuf0": "B", "bcbuf1": "B", "antmp": "B",
        "xn_rot": "A", "xn_rot2": "A", "wkv": "A", "Q_sb": "QB", "dn_st": "D",
        "csn": "A", "xfm_t": "A", "kbfA": "A", "t1A": "A", "appt": "A",
        "wq_st": "Q", "cosownQ": "Q", "sinownQ": "Q", "kbfQ": "Q", "t1Q": "Q",
        "xn_own": "C", "wo_st": "D", "g_D": "D", "sg_D": "D", "xo_D": "D", "out_D": "D",
        "x_t": "AQ", "sq_j": "AQ", "sq_j2": "A", "parts": "AQ", "scol": "AQ",
        "rows_g": "A", "rows_n": "A", "reps": "A",
        "xnb_Q": "Q", "xnp_Q": "Q", "xfm_ow": "Q", "reps_o": "Q", "A_sb": "BCD",
    }
    LIFE["masks"] = "QB"
    names = list(SB)
    for i, a in enumerate(names):
        for b in names[i + 1:]:
            (oa, sa, da), (ob_, sb_, db) = SB[a], SB[b]
            na = _esz(da)
            for s in sa[1:]:
                na *= s
            nb = _esz(db)
            for s in sb_[1:]:
                nb *= s
            if oa < ob_ + nb and ob_ < oa + na:
                if set(LIFE[a]) & set(LIFE[b]):
                    if {a, b} == {"rstd_own", "masks"} or {a, b} == {"nmr_own", "masks"}:
                        continue
                    if {a, b} & {"xnb_Q", "xfm_ow", "reps_o"} and {a, b} & {"x_t", "sq_j", "parts", "scol"}:
                        continue
                    raise AssertionError(f"sbuf overlap {a} / {b}")


def host_layout(inputs):
    Wq = np.ascontiguousarray(np.asarray(inputs["L0_attn_Wq"], dtype=np.float32))
    Wk = np.ascontiguousarray(np.asarray(inputs["L0_attn_Wk"], dtype=np.float32))
    Wv = np.ascontiguousarray(np.asarray(inputs["L0_attn_Wv"], dtype=np.float32))
    Wo = np.ascontiguousarray(np.asarray(inputs["L0_attn_Wo"], dtype=np.float32))
    eo = np.r_[0:64:2, 1:64:2]
    qrows = []
    for m in range(4):
        for i in range(4):
            for h in (8 * m + i, 8 * m + i + 4):
                qrows.extend((h * 64 + eo).tolist())
    wq_t = np.ascontiguousarray(Wq[np.array(qrows)].T)
    krows = []
    for m in range(4):
        for kv in (2 * m, 2 * m + 1):
            krows.extend((kv * 64 + eo).tolist())
    wk_t = np.ascontiguousarray(Wk[np.array(krows)].T)
    wv_t = np.ascontiguousarray(Wv.T)
    wo_t = np.ascontiguousarray(Wo.T)
    gate_t = np.ascontiguousarray(np.asarray(inputs["L0_ffn_gate_w"], dtype=np.float32).T)
    up_t = np.ascontiguousarray(np.asarray(inputs["L0_ffn_up_w"], dtype=np.float32).T)
    down_t = np.ascontiguousarray(np.asarray(inputs["L0_ffn_down_w"], dtype=np.float32).T)
    gate_s2 = np.ascontiguousarray(np.asarray(inputs["L0_ffn_gate_s"], dtype=np.float32).reshape(44, 128).T)
    up_s2 = np.ascontiguousarray(np.asarray(inputs["L0_ffn_up_s"], dtype=np.float32).reshape(44, 128).T)
    down_s2 = np.ascontiguousarray(np.asarray(inputs["L0_ffn_down_s"], dtype=np.float32).reshape(16, 128).T)

    inv_freq = (1.0 / 10000.0 ** (np.arange(0, 64, 2, dtype=np.float32) / 64)).astype(np.float32)
    t = np.arange(T, dtype=np.float32)
    freqs = np.einsum("i,j->ij", t, inv_freq)
    emb = np.concatenate([freqs, freqs], axis=-1)
    cos = np.cos(emb).astype(np.float32)
    sin = np.sin(emb).astype(np.float32)
    cosp = np.concatenate([cos[:, 0::2], cos[:, 1::2]], axis=1).T
    sinp = np.concatenate([-sin[:, 0::2].T, sin[:, 1::2].T], axis=0)
    cosf = np.ascontiguousarray(np.tile(cosp, (2, 1)))
    sinf = np.ascontiguousarray(np.tile(sinp, (2, 1)))

    jj = np.arange(512)
    masks = np.zeros((128, 4 * 512), np.float32)
    for k in range(4):
        masks[:, k * 512:(k + 1) * 512] = (
            jj[None, :] >= (128 * k + np.arange(128))[:, None]).astype(np.float32)
    masks = masks.astype(ml_dtypes.bfloat16)

    swap = np.r_[32:64, 0:32, 96:128, 64:96]
    permM = np.zeros((128, 128), np.float32)
    permM[swap, np.arange(128)] = 1.0
    permM = permM.astype(ml_dtypes.bfloat16)

    x = np.ascontiguousarray(np.asarray(inputs["L0_attn_x"], dtype=np.float32))
    shared = dict(wq_t=wq_t, wk_t=wk_t, wv_t=wv_t, wo_t=wo_t,
                  gate_t=gate_t, up_t=up_t, down_t=down_t,
                  gate_s2=gate_s2, up_s2=up_s2, down_s2=down_s2,
                  cosf=cosf, sinf=sinf, masks=masks, permM=permM,
                  identM=np.eye(128, dtype=np.float32))
    per_core = []
    for c in range(8):
        b, j = c // 4, c % 4
        own = np.r_[512 * j:512 * (j + 1), 512 * (7 - j):512 * (8 - j)]
        xb = x[b]
        d = dict(shared)
        d["x"] = xb
        d["xfm"] = np.ascontiguousarray(xb.T)
        d["xfm_own"] = np.ascontiguousarray(xb.T[:, own])
        d["x_own"] = np.ascontiguousarray(xb[own])
        d["cos_own"] = np.ascontiguousarray(cosf[:, own])
        d["sin_own"] = np.ascontiguousarray(sinf[:, own])
        per_core.append(d)
    return per_core


def build():
    _check_map()
    nc = bass.Bass("TRN2", target_bir_lowering=False)
    t_, v_, c_, g_, y_ = nc.tensor, nc.vector, nc.scalar, nc.gpsimd, nc.sync

    I = {}
    def din(name, shape, dtype=F32):
        I[name] = nc.dram_tensor(name, shape, dtype, kind="ExternalInput").ap()
    din("x", [T, HID]); din("xfm", [HID, T]); din("xfm_own", [HID, 1024])
    din("x_own", [1024, HID]); din("cos_own", [128, 1024]); din("sin_own", [128, 1024])
    din("wq_t", [HID, HID]); din("wk_t", [HID, 512]); din("wv_t", [HID, 512])
    din("wo_t", [HID, HID])
    din("gate_t", [HID, INTER]); din("up_t", [HID, INTER]); din("down_t", [INTER, HID])
    din("gate_s2", [128, 44]); din("up_s2", [128, 44]); din("down_s2", [128, 16])
    din("cosf", [128, T]); din("sinf", [128, T])
    din("masks", [128, 4 * 512], BF16); din("permM", [128, 128], BF16)
    din("identM", [128, 128])
    out = nc.dram_tensor("o", [1024, HID], F32, kind="ExternalOutput").ap()
    xn_dram = nc.dram_tensor("xn_dram", [HID, 1024], BF16, kind="Internal").ap()
    taps = {}
    A = {}
    for name, (off, shape, dt) in SB.items():
        A[name] = nc.alloc_sbuf_tensor_at(name, shape, dt, offset=off).ap()

    psA = nc.alloc_psum_tensor("psA", [128, 2048], F32).ap()
    psB = nc.alloc_psum_tensor("psB", [128, 2048], F32).ap()

    sem = {}
    C = {}
    for n in ["pe", "ac", "ve", "po", "s2", "ss", "sx", "sx1", "scn", "sf", "sf1",
              "sq2", "sr", "sr1", "sw", "sxo", "sod", "sod1", "sout",
              "gk", "gq", "gq1", "ggu", "ggu1", "gdn", "gdn1", "gwo"]:
        sem[n] = nc.semaphore(n).__enter__()
        C[n] = 0

    def inc(instr, s, n=1):
        C[s] += n
        instr.then_inc(sem[s], n)
        return C[s]

    def W(eng, s, val=None):
        if val is None:
            val = C[s]
        if val > 0:
            eng.wait_ge(sem[s], val)

    def dma(q, s, out, in_):
        W(q, s)
        return inc(q.dma_start(out=out, in_=in_), s, 16)

    cnt = {}
    body = nc.body("main", valid_engines=list(mybir.ALL_ENGINES))
    body.__enter__()

    v_.memset(A["c_m8"], -8.0)
    v_.memset(A["c_eps"], 1e-5)
    v_.memset(A["ones64"], 1.0)
    v_.memset(A["ones128"], 1.0)
    vview = A["V_sb"].rearrange("p (s k d) -> p s k d", s=32, k=8)
    v_.memset(A["kbfA"], 0.0)
    v_.memset(A["kbfQ"], 0.0)
    ve_init = inc(v_.memset(vview[:, :, :, 64:65], 1.0), "ve")

    for nm, src in [("identM", "identM"), ("permM", "permM"),
                    ("gateS", "gate_s2"), ("upS", "up_s2"), ("downS", "down_s2")]:
        dma(y_, "ss", A[nm], I[src])
    sd_small = C["ss"]

    wkvv = A["wkv"].rearrange("p (h w) -> p h w", h=16)
    dma(g_, "gk", wkvv[:, :, 0:512], I["wk_t"].rearrange("(h p) o -> p h o", p=128))
    gw_wkv = dma(g_, "gk", wkvv[:, :, 512:1024],
                 I["wv_t"].rearrange("(h p) o -> p h o", p=128))

    def ln_stats(xtile, ti, rows_g, rows_n):
        sc = A["scol"][:, 8 * (ti % 2): 8 * (ti % 2) + 8]
        W(c_, "sx")
        W(c_, "sx1")
        W(c_, "ve")
        c_.activation(A["sq_j"], xtile, AF.Square, accum_out=sc[:, 1:2])
        ac_sq = inc(c_.activation(A["sq_j2"], xtile, AF.Identity, accum_out=sc[:, 0:1]),
                    "ac")
        W(v_, "ac", ac_sq)
        vv = inc(v_.tensor_scalar_mul(sc[:, 2:3], sc[:, 0:1], 1.0 / 2048.0), "ve")
        W(v_, "ve", vv)
        vv = inc(v_.tensor_mul(sc[:, 3:4], sc[:, 2:3], sc[:, 2:3]), "ve")
        W(v_, "ve", vv)
        ve_var = inc(v_.scalar_tensor_tensor(sc[:, 4:5], sc[:, 1:2], 1.0 / 2048.0,
                                             sc[:, 3:4], ALU.mult, ALU.subtract), "ve")
        W(c_, "ve", ve_var)
        vv = inc(c_.activation(sc[:, 5:6], sc[:, 4:5], AF.Ln, bias=A["c_eps"]), "ac")
        W(c_, "ac", vv)
        ac_rstd = inc(c_.activation(sc[:, 6:7], sc[:, 5:6], AF.Exp, scale=-0.5), "ac")
        W(v_, "ac", ac_rstd)
        ve_nmr = inc(v_.scalar_tensor_tensor(sc[:, 7:8], sc[:, 2:3], -1.0, sc[:, 6:7],
                                             ALU.mult, ALU.mult), "ve")
        W(t_, "ve", ve_nmr)
        W(t_, "ss", sd_small)
        t_.transpose(psA[0:1, 0:128], sc[:, 6:7], A["identM"])
        pe_tr = inc(t_.transpose(psA[0:1, 512:640], sc[:, 7:8], A["identM"]), "pe")
        W(v_, "pe", pe_tr)
        v_.tensor_copy(rows_g[0:1, ti * 128:(ti + 1) * 128], psA[0:1, 0:128])
        ve_row = inc(v_.tensor_copy(rows_n[0:1, ti * 128:(ti + 1) * 128],
                                    psA[0:1, 512:640]), "ve")
        return ve_row, ac_sq

    for ti in range(8):
        buf = A["x_t"][:, (ti % 2) * 2048:(ti % 2) * 2048 + 2048]
        if ti >= 2:
            W(y_, "ve", cnt[("row", "own", ti - 2)])
            W(y_, "ac", cnt[("sq", "own", ti - 2)])
        dma(y_, "sx" if ti % 2 == 0 else "sx1", buf, I["x_own"][ti * 128:(ti + 1) * 128, :])
        r, s = ln_stats(buf, ti, A["rstd_own"], A["nmr_own"])
        cnt[("row", "own", ti)] = r
        cnt[("sq", "own", ti)] = s
    ve_rows_own = C["ve"]

    for tg in range(8):
        ve_rows = 0
        for ti in range(4):
            buf = A["x_t"][:, (ti % 2) * 2048:(ti % 2) * 2048 + 2048]
            prev = cnt.get(("row", tg, ti - 2), cnt.get(("row", tg - 1, ti + 2),
                           cnt.get(("row", "own", ti + 6), 0)))
            prevs = cnt.get(("sq", tg, ti - 2), cnt.get(("sq", tg - 1, ti + 2),
                            cnt.get(("sq", "own", ti + 6), 0)))
            W(y_, "ve", prev)
            W(y_, "ac", prevs)
            dma(y_, "sx" if ti % 2 == 0 else "sx1", buf,
                I["x"][tg * 512 + ti * 128: tg * 512 + (ti + 1) * 128, :])
            r, s = ln_stats(buf, ti, A["rows_g"], A["rows_n"])
            cnt[("row", tg, ti)] = r
            cnt[("sq", tg, ti)] = s
            ve_rows = r
        W(y_, "ve", cnt.get(("kadd", tg - 1, 3), 0))
        dma(y_, "scn", A["csn"][:, 0:512], I["cosf"][:, tg * 512:(tg + 1) * 512])
        dma(y_, "scn", A["csn"][:, 512:1024], I["sinf"][:, tg * 512:(tg + 1) * 512])
        sd_csn = C["scn"]
        W(t_, "ve", ve_rows)
        t_.matmul(psA[:, 1024:1536], A["ones128"], A["rows_g"], start=True, stop=True)
        pe_rep = inc(t_.matmul(psA[:, 1536:2048], A["ones128"], A["rows_n"],
                               start=True, stop=True), "pe")
        W(v_, "pe", pe_rep)
        W(v_, "po", cnt.get(("app", tg - 1, NHC - 1), 0))
        v_.tensor_copy(A["reps"][:, 0:512], psA[:, 1024:1536])
        ve_rep = inc(v_.tensor_copy(A["reps"][:, 512:1024], psA[:, 1536:2048]), "ve")
        xnr = A["xn_rot"] if tg % 2 == 0 else A["xn_rot2"]
        xfmv = I["xfm"].rearrange("(c p) t -> p c t", p=128)
        for hp in range(8):
            fb = A["xfm_t"][:, (hp % 2) * 1024:(hp % 2) * 1024 + 1024]
            prev = cnt.get(("appm", tg, 2 * hp - 3), cnt.get(("appm", tg - 1, 12 + 2 * hp + 1), 0))
            W(y_, "ve", prev)
            dma(y_, "sf" if hp % 2 == 0 else "sf1",
                fb.rearrange("p (c t) -> p c t", c=2),
                xfmv[:, 2 * hp:2 * hp + 2, tg * 512:(tg + 1) * 512])
            W(v_, "sf" if hp % 2 == 0 else "sf1")
            for sub in range(2):
                h = 2 * hp + sub
                if h == 0:
                    W(v_, "ve", ve_rep)
                ab = A["appt"][:, (h % 2) * 512:(h % 2) * 512 + 512]
                if h >= 2:
                    W(v_, "po", cnt.get(("app", tg, h - 2), 0))
                cnt[("appm", tg, h)] = inc(
                    v_.tensor_mul(ab, fb[:, sub * 512:(sub + 1) * 512],
                                  A["reps"][:, 0:512]), "ve")
                W(g_, "ve", cnt[("appm", tg, h)])
                if h == 0:
                    W(g_, "pe", cnt.get(("proj", tg - 2), 0))
                cnt[("app", tg, h)] = inc(
                    g_.tensor_add(xnr[:, h * 512:(h + 1) * 512], ab,
                                  A["reps"][:, 512:1024]), "po")
        W(t_, "gk", gw_wkv)
        W(t_, "po", cnt[("app", tg, NHC - 1)])
        W(t_, "ve", cnt.get(("kt1", tg - 1, 3), 0))
        for h in range(NHC):
            for ck in range(4):
                i = t_.matmul(psB[:, ck * 512:(ck + 1) * 512],
                              A["wkv"][:, h * 1024 + ck * 128: h * 1024 + (ck + 1) * 128],
                              xnr[:, h * 512:(h + 1) * 512],
                              start=(h == 0), stop=(h == NHC - 1))
        pe_K = inc(i, "pe")
        for ck in range(4):
            kb = A["kbfA"][:, (ck % 2) * 512:(ck % 2) * 512 + 512]
            W(v_, "pe", pe_K)
            W(v_, "pe", cnt.get(("ksw", tg, ck - 2), 0))
            ve_kb = inc(v_.tensor_copy(kb, psB[:, ck * 512:(ck + 1) * 512]), "ve")
            W(t_, "ve", ve_kb)
            W(t_, "ve", cnt.get(("kadd", tg, ck - 2), 0))
            cnt[("ksw", tg, ck)] = inc(
                t_.matmul(psA[:, (ck % 2) * 512:(ck % 2) * 512 + 512], A["permM"], kb,
                          start=True, stop=True), "pe")
            W(v_, "scn", sd_csn)
            W(v_, "pe", cnt[("ksw", tg, ck)])
            tb = A["t1A"][:, (ck % 2) * 512:(ck % 2) * 512 + 512]
            cnt[("kt1", tg, ck)] = inc(
                v_.tensor_mul(tb, psB[:, ck * 512:(ck + 1) * 512], A["csn"][:, 0:512]), "ve")
            vv = inc(v_.tensor_mul(psA[:, (2 + ck % 2) * 512:(2 + ck % 2) * 512 + 512],
                                   psA[:, (ck % 2) * 512:(ck % 2) * 512 + 512],
                                   A["csn"][:, 512:1024]), "ve")
            W(v_, "ve", vv)
            cnt[("kadd", tg, ck)] = inc(
                v_.tensor_add(A["K_sb"][:, ck * 4096 + tg * 512: ck * 4096 + (tg + 1) * 512],
                              tb, psA[:, (2 + ck % 2) * 512:(2 + ck % 2) * 512 + 512]), "ve")
        W(t_, "ve", cnt[("kadd", tg, 3)])
        W(t_, "ve", cnt.get(("vev", tg - 1, 3), 0))
        for h in range(NHC):
            for tv in range(4):
                i = t_.matmul(psA[:, tv * 512:(tv + 1) * 512],
                              xnr[:, h * 512 + tv * 128: h * 512 + (tv + 1) * 128],
                              A["wkv"][:, h * 1024 + 512: h * 1024 + 1024],
                              start=(h == 0), stop=(h == NHC - 1))
        pe_V = inc(i, "pe")
        cnt[("proj", tg)] = C["pe"]
        W(v_, "pe", pe_V)
        if tg == 0:
            W(v_, "ve", ve_init)
        for tv in range(4):
            dst = vview[:, tg * 4 + tv, :, 0:64]
            cnt[("vev", tg, tv)] = inc(
                v_.tensor_copy(dst, psA[:, tv * 512:(tv + 1) * 512]
                               .rearrange("p (k d) -> p k d", k=8)), "ve")

    ve_A_end = C["ve"]

    W(t_, "ve", ve_A_end)
    for half in range(2):
        t_.matmul(psA[:, half * 512:(half + 1) * 512], A["ones128"],
                  A["rstd_own"][0:1, half * 512:(half + 1) * 512], start=True, stop=True)
        i = t_.matmul(psA[:, 1024 + half * 512: 1024 + (half + 1) * 512], A["ones128"],
                      A["nmr_own"][0:1, half * 512:(half + 1) * 512], start=True, stop=True)
    pe_orep = inc(i, "pe")
    W(v_, "pe", pe_orep)
    v_.tensor_copy(A["reps_o"][:, 0:1024], psA[:, 0:1024])
    ve_orep = inc(v_.tensor_copy(A["reps_o"][:, 1024:2048], psA[:, 1024:2048]), "ve")

    for h in range(NHC):
        fb = A["xfm_ow"][:, (h % 2) * 1024:(h % 2) * 1024 + 1024]
        if h < 2:
            W(y_, "ve", cnt[("row", 7, 3)])
            W(y_, "ac", cnt[("sq", 7, 3)])
        W(y_, "ve", cnt.get(("oxn", h - 2), 0))
        dma(y_, "sf" if h % 2 == 0 else "sf1", fb, I["xfm_own"][h * 128:(h + 1) * 128, :])
        W(v_, "sf" if h % 2 == 0 else "sf1")
        if h == 0:
            W(v_, "ve", ve_orep)
        W(v_, "sw", cnt.get(("oxnd", h - 1), 0))
        vv = inc(v_.tensor_mul(A["t1Q"], fb, A["reps_o"][:, 0:1024]), "ve")
        W(v_, "ve", vv)
        cnt[("oxn", h)] = inc(v_.tensor_add(A["xnb_Q"][:, 0:1024], A["t1Q"],
                                            A["reps_o"][:, 1024:2048]), "ve")
        W(y_, "ve", cnt[("oxn", h)])
        dma(y_, "sw", xn_dram[h * 128:(h + 1) * 128, :], A["xnb_Q"][:, 0:1024])
        cnt[("oxnd", h)] = C["sw"]
    ou_xn = C["sw"]

    W(y_, "ve", cnt[("kadd", 7, 3)])
    dma(y_, "sq2", A["cosownQ"], I["cos_own"])
    dma(y_, "sq2", A["sinownQ"], I["sin_own"])
    sd_csown = C["sq2"]
    W(y_, "ve", cnt[("oxn", NHC - 1)])
    dma(y_, "sq2", A["masks"], I["masks"])
    sd_masks = C["sq2"]

    wqv = A["wq_st"].rearrange("p (b h w) -> p b h w", b=2, h=16)
    for op8 in range(8):
        if op8 < 2:
            W(g_, "pe", cnt[("proj", 7)])
        W(g_, "pe", cnt.get(("qmm", op8 - 2, NHC - 1), 0))
        gw_wq = dma(g_, "gq" if op8 % 2 == 0 else "gq1", wqv[:, op8 % 2],
                    I["wq_t"].rearrange("(h p) o -> p h o", p=128)[:, :, op8 * 256:(op8 + 1) * 256])
        xndv = xn_dram.rearrange("(c p) t -> p c t", p=128)
        for hp in range(8):
            xbp = A["xnp_Q"][:, (hp % 2) * 2048:(hp % 2) * 2048 + 2048]
            prev = cnt.get(("qmm", op8, 2 * hp - 3),
                           cnt.get(("qmm", op8 - 1, 13 + 2 * hp), 0))
            W(y_, "pe", prev)
            W(y_, "sw", 16 * (2 * hp + 2))
            dma(y_, "sr" if hp % 2 == 0 else "sr1",
                xbp.rearrange("p (c t) -> p c t", c=2), xndv[:, 2 * hp:2 * hp + 2, :])
            W(t_, "sr" if hp % 2 == 0 else "sr1")
            for sub in range(2):
                h = 2 * hp + sub
                xb = xbp[:, sub * 1024:(sub + 1) * 1024]
                if h == 0:
                    W(t_, "gq" if op8 % 2 == 0 else "gq1", gw_wq)
                    W(t_, "ve", cnt.get(("qrope", op8 - 1), ve_orep))
                for oc in range(2):
                    for th in range(2):
                        i = t_.matmul(psA[:, (oc * 2 + th) * 512:(oc * 2 + th + 1) * 512],
                                      wqv[:, op8 % 2, h, oc * 128:(oc + 1) * 128],
                                      xb[:, th * 512:(th + 1) * 512],
                                      start=(h == 0), stop=(h == NHC - 1))
                cnt[("qmm", op8, h)] = inc(i, "pe")
        pe_q = cnt[("qmm", op8, NHC - 1)]
        for oc in range(2):
            ch = op8 * 2 + oc
            W(v_, "pe", pe_q)
            W(v_, "pe", cnt.get(("qsw", ch - 1), 0))
            ve_kb = inc(v_.tensor_copy(A["kbfQ"], psA[:, oc * 1024:(oc + 1) * 1024]), "ve")
            W(t_, "ve", ve_kb)
            if ch == 0:
                W(t_, "ve", ve_A_end)
            else:
                W(t_, "ve", cnt.get(("qadd", ch - 1), 0))
            t_.matmul(psB[:, 0:512], A["permM"], A["kbfQ"][:, 0:512], start=True, stop=True)
            cnt[("qsw", ch)] = inc(
                t_.matmul(psB[:, 512:1024], A["permM"], A["kbfQ"][:, 512:1024],
                          start=True, stop=True), "pe")
            W(v_, "sq2", sd_csown)
            W(v_, "pe", cnt[("qsw", ch)])
            vv = inc(v_.tensor_mul(A["t1Q"], psA[:, oc * 1024:(oc + 1) * 1024],
                                   A["cosownQ"]), "ve")
            vv = inc(v_.tensor_mul(psB[:, 1024:2048], psB[:, 0:1024], A["sinownQ"]), "ve")
            W(v_, "ve", vv)
            cnt[("qadd", ch)] = inc(
                v_.tensor_add(A["Q_sb"][:, ch * 1024:(ch + 1) * 1024], A["t1Q"],
                              psB[:, 1024:2048]), "ve")
        cnt[("qrope", op8)] = C["ve"]

    ve_Q_end = C["ve"]
    pe_Q_end = C["pe"]
    ac_Q_end = C["ac"]
    pid_t = t_.compute_val(t_.partition_id() % 4)
    pid_c = c_.compute_val(c_.partition_id() % 4)
    pid_v = v_.compute_val(v_.partition_id() % 4)
    body.__exit__(None, None, None)

    snap = dict(C)
    final_counts = None
    for var in nc.Switch(engines=[t_, c_, v_], index=[pid_t, pid_c, pid_v], n=4):
        C.clear(); C.update(snap)
        j = var
        nodd = 0
        cnt_s2 = [0]
        avfree_next = [0]
        firstL = [True]
        vmask_first = True
        exp_cnt = {}
        for gi, g in enumerate([j, 7 - j]):
            nsb = 4 * (g + 1)
            for ch in range(16):
                m = ch // 4
                av_free = avfree_next[0] if avfree_next[0] else C["ve"]
                pe_av = 0
                qk_cnt = {}

                def emit_qk(sb):
                    par = sb % 2
                    if (ch, sb - 2) in exp_cnt:
                        W(t_, "ac", exp_cnt[(ch, sb - 2)])
                    else:
                        W(t_, "ac")
                    for hh in range(2):
                        b0 = 64 * hh
                        i = t_.matmul(
                            psA[:, par * 1024 + hh * 512: par * 1024 + (hh + 1) * 512],
                            A["K_sb"][b0:b0 + 64,
                                      m * 4096 + sb * 128: m * 4096 + (sb + 1) * 128],
                            A["Q_sb"][b0:b0 + 64,
                                      ch * 1024 + gi * 512: ch * 1024 + (gi + 1) * 512],
                            start=True, stop=True)
                    qk_cnt[sb] = inc(i, "pe")

                if firstL[0]:
                    W(t_, "ve", ve_Q_end)
                    W(t_, "ac", ac_Q_end)
                    firstL[0] = False
                emit_qk(0)
                for sb in range(nsb):
                    par = sb % 2
                    if sb + 1 < nsb:
                        emit_qk(sb + 1)
                    W(c_, "pe", qk_cnt[sb])
                    eb = A["Ebuf"][:, (sb % 2) * 1024:(sb % 2) * 1024 + 1024]
                    ac_exp = inc(c_.activation(eb, psA[:, par * 1024:(par + 1) * 1024],
                                               AF.Exp, bias=A["c_m8"], scale=0.125), "ac")
                    exp_cnt[(ch, sb)] = ac_exp
                    diag = sb >= 4 * g
                    if diag:
                        k = sb - 4 * g
                        if vmask_first:
                            W(v_, "sq2", sd_masks)
                            vmask_first = False
                        W(v_, "ac", ac_exp)
                        W(v_, "pe")
                        for hh in range(2):
                            i = v_.tensor_mul(A["Emask"][:, hh * 512:(hh + 1) * 512],
                                              eb[:, hh * 512:(hh + 1) * 512],
                                              A["masks"][:, k * 512:(k + 1) * 512])
                        ve_mask = inc(i, "ve")
                        W(t_, "ve", ve_mask)
                        src = A["Emask"]
                    else:
                        W(t_, "ac", ac_exp)
                        src = eb
                    if sb == 0:
                        W(t_, "ve", av_free)
                        W(t_, "s2", cnt_s2[0])
                    for hh in range(2):
                        kv = 2 * m + hh
                        i = t_.matmul(
                            psB[0:65, hh * 512:(hh + 1) * 512],
                            A["V_sb"].rearrange("p (s w) -> p s w", s=32)
                            [:, sb, kv * 65:(kv + 1) * 65],
                            src[:, hh * 512:(hh + 1) * 512],
                            start=(sb == 0), stop=(sb == nsb - 1))
                    pe_av = inc(i, "pe")
                W(v_, "pe", pe_av)
                ve_nd = inc(v_.tensor_copy(A["nd_sb"], psB[0:65, 0:1024]), "ve")
                avfree_next[0] = ve_nd
                for hh in range(2):
                    head = 8 * (ch // 4) + (ch % 4) + 4 * hh
                    W(v_, "ve", ve_nd)
                    W(v_, "pe")
                    ve_rd = inc(v_.reciprocal(A["rdbuf"],
                                              A["nd_sb"][64:65, hh * 512:(hh + 1) * 512]),
                                "ve")
                    W(t_, "ve", ve_rd)
                    i = t_.matmul(psB[0:64, 1024 + hh * 512: 1024 + hh * 512 + 512],
                                  A["ones64"], A["rdbuf"], start=True, stop=True)
                    pe_bc = inc(i, "pe")
                    bcb = A["bcbuf0"] if hh == 0 else A["bcbuf1"]
                    W(v_, "pe", pe_bc)
                    ve_bc = inc(v_.tensor_copy(bcb, psB[0:64, 1024 + hh * 512:
                                                        1024 + hh * 512 + 512]), "ve")
                    W(v_, "ve", ve_bc)
                    dst_cols = slice((head // 2) * 1024 + gi * 512,
                                     (head // 2) * 1024 + (gi + 1) * 512)
                    if head % 2 == 0:
                        i = v_.tensor_mul(A["A_sb"][0:64, dst_cols],
                                          A["nd_sb"][0:64, hh * 512:(hh + 1) * 512], bcb)
                        inc(i, "ve")
                    else:
                        par2 = nodd % 2
                        W(v_, "s2", cnt_s2[0] - 16 if nodd >= 2 else 0)
                        i = v_.tensor_mul(A["antmp"][:, par2 * 512:(par2 + 1) * 512],
                                          A["nd_sb"][0:64, hh * 512:(hh + 1) * 512], bcb)
                        ve_an = inc(i, "ve")
                        W(c_, "ve", ve_an)
                        W(c_, "s2")
                        inc(c_.dma_start(out=A["A_sb"][64:128, dst_cols],
                                         in_=A["antmp"][:, par2 * 512:(par2 + 1) * 512]),
                            "s2", 16)
                        cnt_s2[0] = C["s2"]
                        nodd += 1
        if final_counts is None:
            final_counts = dict(C)
        else:
            assert final_counts == C, "switch variant sem divergence"
    C.clear(); C.update(final_counts)

    body2 = nc.body("tail", valid_engines=list(mybir.ALL_ENGINES))
    body2.__enter__()
    ve_B_end = C["ve"]
    pe_B_end = C["pe"]
    ac_B_end = C["ac"]
    s2_B_end = C["s2"]

    for h in range(NHC):
        if h == 0:
            W(y_, "pe", pe_Q_end)
            W(y_, "ve", ve_Q_end)
        dma(y_, "sxo", A["xn_own"][:, h * 1024:(h + 1) * 1024],
            xn_dram[h * 128:(h + 1) * 128, :])
    sd_xno = C["sxo"]

    guv_g = A["gu_g"].rearrange("p (b h w) -> p b h w", b=2, h=16)
    guv_u = A["gu_u"].rearrange("p (b h w) -> p b h w", b=2, h=16)
    gatev = I["gate_t"].rearrange("(h p) i -> p h i", p=128)
    upv = I["up_t"].rearrange("(h p) i -> p h i", p=128)
    for ic in range(NIC):
        if ic < 2:
            W(g_, "ve", ve_B_end)
        W(g_, "pe", cnt.get(("fmm", ic - 2, NHC - 1), 0))
        gs_ = "ggu" if ic % 2 == 0 else "ggu1"
        dma(g_, gs_, guv_g[:, ic % 2], gatev[:, :, ic * 128:(ic + 1) * 128])
        gw_gu = dma(g_, gs_, guv_u[:, ic % 2], upv[:, :, ic * 128:(ic + 1) * 128])
        ps = psA if ic % 2 == 0 else psB
        for h in range(NHC):
            if h == 0:
                W(t_, "sxo", sd_xno)
                W(t_, "ggu" if ic % 2 == 0 else "ggu1", gw_gu)
                if ic < 2:
                    W(t_, "ve", ve_B_end)
                    W(t_, "ac", ac_B_end)
                else:
                    W(t_, "ac", cnt.get(("fsil", ic - 2), 0))
                    W(t_, "ve", cnt.get(("fH", ic - 2), 0))
            for th in range(2):
                t_.matmul(ps[:, th * 512:(th + 1) * 512],
                          guv_g[:, ic % 2, h, :],
                          A["xn_own"][:, h * 1024 + th * 512: h * 1024 + (th + 1) * 512],
                          start=(h == 0), stop=(h == NHC - 1))
            for th in range(2):
                i = t_.matmul(ps[:, 1024 + th * 512: 1024 + (th + 1) * 512],
                              guv_u[:, ic % 2, h, :],
                              A["xn_own"][:, h * 1024 + th * 512: h * 1024 + (th + 1) * 512],
                              start=(h == 0), stop=(h == NHC - 1))
        cnt[("fmm", ic, NHC - 1)] = inc(i, "pe")
        W(c_, "pe", cnt[("fmm", ic, NHC - 1)])
        W(c_, "ve", cnt.get(("fH", ic - 1), 0))
        cnt[("fsil", ic)] = inc(
            c_.activation(A["gtC"], ps[:, 0:1024], AF.Silu,
                          scale=A["gateS"][:, ic:ic + 1]), "ac")
        W(v_, "ac", cnt[("fsil", ic)])
        if ic == 0:
            W(v_, "s2", s2_B_end)
        cnt[("fH", ic)] = inc(
            v_.scalar_tensor_tensor(A["H_sb"][:, ic * 1024:(ic + 1) * 1024],
                                    ps[:, 1024:2048], A["upS"][:, ic:ic + 1], A["gtC"],
                                    ALU.mult, ALU.mult), "ve")
    ve_C_end = C["ve"]
    pe_C_end = C["pe"]

    dnv = A["dn_st"].rearrange("p (b i w) -> p b i w", b=2, i=22)
    wov = A["wo_st"].rearrange("p (b h w) -> p b h w", b=2, h=16)
    downv = I["down_t"].rearrange("(i p) o -> p i o", p=128)
    wodv = I["wo_t"].rearrange("(h p) o -> p h o", p=128)
    for ob in range(8):
        for half in range(2):
            if ob == 0:
                W(g_, "pe", pe_C_end)
            W(g_, "pe", cnt.get(("dmm", ob - 1, half), 0))
            cnt[("gwdn", ob, half)] = dma(
                g_, "gdn" if half == 0 else "gdn1", dnv[:, half],
                downv[:, half * 22:(half + 1) * 22, ob * 256:(ob + 1) * 256])
        if ob == 0:
            W(g_, "pe", pe_C_end)
        W(g_, "pe", cnt.get(("wmm", ob - 2), 0))
        cnt[("gwwo", ob)] = dma(g_, "gwo", wov[:, ob % 2],
                                wodv[:, :, ob * 256:(ob + 1) * 256])
        for i44 in range(NIC):
            half, idx = (0, i44) if i44 < 22 else (1, i44 - 22)
            if i44 == 0:
                W(t_, "ve", ve_C_end)
                W(t_, "gdn", cnt[("gwdn", ob, 0)])
                W(t_, "ac", cnt.get(("dg", ob - 1, 1), 0))
                W(t_, "ve", cnt.get(("dadd", ob - 1, 1, 7), 0))
            if i44 == 22:
                W(t_, "gdn1", cnt[("gwdn", ob, 1)])
            for oc in range(2):
                for th in range(2):
                    i = t_.matmul(psA[:, (oc * 2 + th) * 512:(oc * 2 + th + 1) * 512],
                                  dnv[:, half, idx, oc * 128:(oc + 1) * 128],
                                  A["H_sb"][:, i44 * 1024 + th * 512:
                                            i44 * 1024 + (th + 1) * 512],
                                  start=(i44 == 0), stop=(i44 == NIC - 1))
            if i44 == 21:
                cnt[("dmm", ob, 0)] = inc(i, "pe")
        cnt[("dmm", ob, 1)] = pe_dn = inc(i, "pe")
        for h in range(NHC):
            if h == 0:
                W(t_, "gwo", cnt[("gwwo", ob)])
                W(t_, "ve", ve_B_end)
                W(t_, "s2", s2_B_end)
                W(t_, "ve", cnt.get(("dsg", ob - 1, 1), 0))
            for oc in range(2):
                for th in range(2):
                    i = t_.matmul(psB[:, (oc * 2 + th) * 512:(oc * 2 + th + 1) * 512],
                                  wov[:, ob % 2, h, oc * 128:(oc + 1) * 128],
                                  A["A_sb"][:, h * 1024 + th * 512:
                                            h * 1024 + (th + 1) * 512],
                                  start=(h == 0), stop=(h == NHC - 1))
        cnt[("wmm", ob)] = pe_wo = inc(i, "pe")
        for oc in range(2):
            occ = 2 * ob + oc
            W(c_, "pe", pe_dn)
            W(c_, "ve", cnt.get(("dsg", ob, oc - 1), cnt.get(("dsg", ob - 1, 1), 0)))
            cnt[("dg", ob, oc)] = inc(
                c_.activation(A["g_D"], psA[:, oc * 1024:(oc + 1) * 1024], AF.Silu,
                              scale=A["downS"][:, occ:occ + 1]), "ac")
            W(v_, "pe", pe_wo)
            W(v_, "ac", cnt[("dg", ob, oc)])
            cnt[("dsg", ob, oc)] = inc(
                v_.tensor_mul(A["sg_D"][:, oc * 1024:(oc + 1) * 1024],
                              psB[:, oc * 1024:(oc + 1) * 1024], A["g_D"]), "ve")
        for tt in range(8):
            xo = A["xo_D"][:, (tt % 2) * 256:(tt % 2) * 256 + 256]
            if (ob, tt) == (0, 0):
                W(y_, "pe", pe_C_end)
            W(y_, "ve", cnt.get(("dadd", ob, 1, tt - 2),
                                cnt.get(("dadd", ob - 1, 1, 6 + tt), 0)))
            sods = "sod" if tt % 2 == 0 else "sod1"
            dma(y_, sods, xo, I["x_own"][tt * 128:(tt + 1) * 128,
                                         ob * 256:(ob + 1) * 256])
            sd_xo = C[sods]
            for oc in range(2):
                occ = 2 * ob + oc
                W(t_, "ve", cnt[("dsg", ob, 1)])
                W(t_, "ac", cnt[("dg", ob, 1)])
                W(t_, "ve", cnt.get(("dadd", ob, oc, tt - 2),
                                    cnt.get(("dadd", ob - 1, oc, 6 + tt), 0)))
                bank = 2 * (tt % 2) + oc
                i = t_.transpose(psA[:, bank * 512: bank * 512 + 128],
                                 A["sg_D"][:, oc * 1024 + tt * 128:
                                           oc * 1024 + (tt + 1) * 128], A["identM"])
                pe_tr = inc(i, "pe")
                W(v_, "pe", pe_tr)
                W(v_, sods, sd_xo)
                W(v_, "sout", cnt.get(("dout", ob, tt - 4),
                                      cnt.get(("dout", ob - 1, 4 + tt), 0)))
                cnt[("dadd", ob, oc, tt)] = inc(
                    v_.tensor_add(A["out_D"][:, (tt % 4) * 256 + oc * 128:
                                             (tt % 4) * 256 + oc * 128 + 128],
                                  psA[:, bank * 512: bank * 512 + 128],
                                  xo[:, oc * 128:(oc + 1) * 128]), "ve")
            W(y_, "ve", cnt[("dadd", ob, 1, tt)])
            dma(y_, "sout", out[tt * 128:(tt + 1) * 128, ob * 256:(ob + 1) * 256],
                A["out_D"][:, (tt % 4) * 256:(tt % 4) * 256 + 256])
            cnt[("dout", ob, tt)] = C["sout"]

    W(y_, "sout", C["sout"])
    body2.__exit__(None, None, None)
    return nc


def kernel(**inputs):
    per_core = host_layout(inputs)
    nc = build()
    res = run_bass_kernel_spmd(nc, per_core, core_ids=list(range(8)))
    outp = np.zeros((2, 4096, 2048), np.float32)
    for c in range(8):
        b, j = c // 4, c % 4
        r = np.asarray(res.results[c]["o"], np.float32)
        outp[b, 512 * j:512 * (j + 1)] = r[0:512]
        outp[b, 512 * (7 - j):512 * (8 - j)] = r[512:1024]
    return outp


if __name__ == "__main__":
    pass


# version 08 (final)
# speedup vs baseline: 1.3155x; 1.0071x over previous
import numpy as np
import ml_dtypes
import concourse.bass as bass
import concourse.mybir as mybir
from concourse.bass_utils import run_bass_kernel_spmd

F32 = mybir.dt.float32
BF16 = mybir.dt.bfloat16
AF = mybir.ActivationFunctionType
ALU = mybir.AluOpType
AX = mybir.AxisListType

T, HID, INTER = 4096, 2048, 5632
NHC, NIC = 16, 44
KB = 1024

SB = {}
SB_BASE = 13 * KB
def _sb(name, off_kb, shape, dtype):
    assert name not in SB
    off = int(off_kb * KB) + SB_BASE
    off = (off + 31) // 32 * 32
    SB[name] = (off, shape, dtype)

_sb("c_m8",     4.000, [128, 1], F32)
_sb("c_eps",    4.0625, [128, 1], F32)
_sb("ones64",   4.125, [1, 64], F32)
_sb("ones128",  4.375, [1, 128], F32)
_sb("gateS",    4.875, [128, 44], F32)
_sb("upS",      5.0625, [128, 44], F32)
_sb("downS",    5.25, [128, 16], F32)
_sb("permM",    5.3125, [128, 128], BF16)
_sb("identM",   5.5625, [128, 128], F32)
_sb("rstd_own", 6.25, [1, 1024], F32)
_sb("nmr_own", 10.25, [1, 1024], F32)
_sb("masks",    14.25, [128, 4 * 512], BF16)
_sb("gu_g",   110.0, [128, 2 * 16 * 128], BF16)
_sb("gu_u",   118.0, [128, 2 * 16 * 128], BF16)
_sb("K_sb",    22.0, [128, 4 * 4096], BF16)
_sb("V_sb",    54.0, [128, 32 * 520], BF16)
_sb("H_sb",    22.0, [128, 44 * 1024], BF16)
_sb("Emask",   88.0, [128, 2048], BF16)
_sb("gtC",    132.5, [128, 1024], F32)
_sb("Ebuf",    92.5, [128, 2048], BF16)
_sb("nd_sb",   96.5, [65, 1024], F32)
_sb("rdbuf",  101.0, [1, 512], F32)
_sb("bcbuf0", 103.5, [64, 512], F32)
_sb("bcbuf1", 105.5, [64, 512], F32)
_sb("antmp",  107.5, [64, 2 * 512], BF16)
_sb("xn_rot", 110.0, [128, 16 * 512], BF16)
_sb("xn_rot2", 92.5, [128, 16 * 512], BF16)
_sb("wkv",    126.0, [128, 16 * 1024], BF16)
_sb("Q_sb",   110.0, [128, 16 * 1024], BF16)
_sb("dn_st",  110.0, [128, 2 * 22 * 256], BF16)
_sb("csn",    158.0, [128, 2 * 512], F32)
_sb("xfm_t",  162.0, [128, 2 * 1024], F32)
_sb("kbfA",   170.0, [128, 2 * 512], BF16)
_sb("t1A",    172.0, [128, 2 * 512], F32)
_sb("appt",   176.0, [128, 2 * 512], F32)
_sb("wq_st",  142.5, [128, 2 * 16 * 256], BF16)
_sb("cosownQ", 88.0, [128, 1024], F32)
_sb("sinownQ", 92.5, [128, 1024], F32)
_sb("kbfQ",    96.5, [128, 1024], BF16)
_sb("t1Q",    158.6, [128, 1024], F32)
_sb("xn_own", 142.5, [128, 16 * 1024], BF16)
_sb("wo_st",  142.5, [128, 2 * 16 * 256], BF16)
_sb("g_D",    158.5, [128, 1024], F32)
_sb("sg_D",   162.5, [128, 2 * 1024], F32)
_sb("xo_D",   170.5, [128, 2 * 256], F32)
_sb("out_D",  172.5, [128, 4 * 256], F32)
_sb("x_t",    180.5, [128, 2 * 2048], F32)
_sb("sq_j",   196.5, [128, 2048], BF16)
_sb("sq_j2",   88.0, [128, 2048], BF16)
_sb("parts",  200.6, [128, 16], F32)
_sb("scol",   200.7, [128, 16], F32)
_sb("rows_g", 200.8, [1, 512], F32)
_sb("rows_n", 202.8, [1, 512], F32)
_sb("reps",   204.8, [128, 2 * 512], F32)
_sb("xnb_Q",  176.5, [128, 1024], BF16)
_sb("xnp_Q",   98.5, [128, 2 * 2048], BF16)
_sb("xfm_ow", 182.5, [128, 2 * 1024], F32)
_sb("reps_o", 190.5, [128, 2 * 1024], F32)
_sb("A_sb",   174.5, [128, 16 * 1024], BF16)
SB["out_D"] = (int(170.5 * KB) + SB_BASE, [128, 4 * 256], F32)
SB["xo_D"] = (int(136.5 * KB) + SB_BASE, [128, 2 * 256], F32)
SB_LIMIT = 229 * KB


def _esz(dt):
    return 2 if dt == BF16 else 4


def _check_map():
    for name, (off, shape, dt) in SB.items():
        n = 1
        for s in shape[1:]:
            n *= s
        assert off + n * _esz(dt) <= SB_LIMIT, f"{name} out of sbuf"
    LIFE = {
        "c_m8": "WAQBCD", "c_eps": "WAQBCD", "ones64": "WAQBCD", "ones128": "WAQBCD",
        "gateS": "WAQBCD", "upS": "WAQBCD", "downS": "WAQBCD", "permM": "WAQBCD",
        "identM": "WAQBCD",
        "rstd_own": "AQ", "nmr_own": "AQ", "masks": "B", "gu_g": "C", "gu_u": "C",
        "K_sb": "AQB", "V_sb": "AQB", "H_sb": "CD",
        "Emask": "B", "gtC": "C", "Ebuf": "B", "nd_sb": "B", "rdbuf": "B", "bcbuf0": "B", "bcbuf1": "B", "antmp": "B",
        "xn_rot": "A", "xn_rot2": "A", "wkv": "A", "Q_sb": "QB", "dn_st": "D",
        "csn": "A", "xfm_t": "A", "kbfA": "A", "t1A": "A", "appt": "A",
        "wq_st": "Q", "cosownQ": "Q", "sinownQ": "Q", "kbfQ": "Q", "t1Q": "Q",
        "xn_own": "C", "wo_st": "D", "g_D": "D", "sg_D": "D", "xo_D": "D", "out_D": "D",
        "x_t": "AQ", "sq_j": "AQ", "sq_j2": "A", "parts": "AQ", "scol": "AQ",
        "rows_g": "A", "rows_n": "A", "reps": "A",
        "xnb_Q": "Q", "xnp_Q": "Q", "xfm_ow": "Q", "reps_o": "Q", "A_sb": "BCD",
    }
    LIFE["masks"] = "QB"
    names = list(SB)
    for i, a in enumerate(names):
        for b in names[i + 1:]:
            (oa, sa, da), (ob_, sb_, db) = SB[a], SB[b]
            na = _esz(da)
            for s in sa[1:]:
                na *= s
            nb = _esz(db)
            for s in sb_[1:]:
                nb *= s
            if oa < ob_ + nb and ob_ < oa + na:
                if set(LIFE[a]) & set(LIFE[b]):
                    if {a, b} == {"rstd_own", "masks"} or {a, b} == {"nmr_own", "masks"}:
                        continue
                    if {a, b} & {"xnb_Q", "xfm_ow", "reps_o"} and {a, b} & {"x_t", "sq_j", "parts", "scol"}:
                        continue
                    raise AssertionError(f"sbuf overlap {a} / {b}")


def host_layout(inputs):
    Wq = np.ascontiguousarray(np.asarray(inputs["L0_attn_Wq"], dtype=np.float32))
    Wk = np.ascontiguousarray(np.asarray(inputs["L0_attn_Wk"], dtype=np.float32))
    Wv = np.ascontiguousarray(np.asarray(inputs["L0_attn_Wv"], dtype=np.float32))
    Wo = np.ascontiguousarray(np.asarray(inputs["L0_attn_Wo"], dtype=np.float32))
    eo = np.r_[0:64:2, 1:64:2]
    qrows = []
    for m in range(4):
        for i in range(4):
            for h in (8 * m + i, 8 * m + i + 4):
                qrows.extend((h * 64 + eo).tolist())
    wq_t = np.ascontiguousarray(Wq[np.array(qrows)].T)
    krows = []
    for m in range(4):
        for kv in (2 * m, 2 * m + 1):
            krows.extend((kv * 64 + eo).tolist())
    wk_t = np.ascontiguousarray(Wk[np.array(krows)].T)
    wv_t = np.ascontiguousarray(Wv.T)
    wo_t = np.ascontiguousarray(Wo.T)
    gate_t = np.ascontiguousarray(np.asarray(inputs["L0_ffn_gate_w"], dtype=np.float32).T)
    up_t = np.ascontiguousarray(np.asarray(inputs["L0_ffn_up_w"], dtype=np.float32).T)
    down_t = np.ascontiguousarray(np.asarray(inputs["L0_ffn_down_w"], dtype=np.float32).T)
    gate_s2 = np.ascontiguousarray(np.asarray(inputs["L0_ffn_gate_s"], dtype=np.float32).reshape(44, 128).T)
    up_s2 = np.ascontiguousarray(np.asarray(inputs["L0_ffn_up_s"], dtype=np.float32).reshape(44, 128).T)
    down_s2 = np.ascontiguousarray(np.asarray(inputs["L0_ffn_down_s"], dtype=np.float32).reshape(16, 128).T)

    inv_freq = (1.0 / 10000.0 ** (np.arange(0, 64, 2, dtype=np.float32) / 64)).astype(np.float32)
    t = np.arange(T, dtype=np.float32)
    freqs = np.einsum("i,j->ij", t, inv_freq)
    emb = np.concatenate([freqs, freqs], axis=-1)
    cos = np.cos(emb).astype(np.float32)
    sin = np.sin(emb).astype(np.float32)
    cosp = np.concatenate([cos[:, 0::2], cos[:, 1::2]], axis=1).T
    sinp = np.concatenate([-sin[:, 0::2].T, sin[:, 1::2].T], axis=0)
    cosf = np.ascontiguousarray(np.tile(cosp, (2, 1)))
    sinf = np.ascontiguousarray(np.tile(sinp, (2, 1)))

    jj = np.arange(512)
    masks = np.zeros((128, 4 * 512), np.float32)
    for k in range(4):
        masks[:, k * 512:(k + 1) * 512] = (
            jj[None, :] >= (128 * k + np.arange(128))[:, None]).astype(np.float32)
    masks = masks.astype(ml_dtypes.bfloat16)

    swap = np.r_[32:64, 0:32, 96:128, 64:96]
    permM = np.zeros((128, 128), np.float32)
    permM[swap, np.arange(128)] = 1.0
    permM = permM.astype(ml_dtypes.bfloat16)

    x = np.ascontiguousarray(np.asarray(inputs["L0_attn_x"], dtype=np.float32))
    shared = dict(wq_t=wq_t, wk_t=wk_t, wv_t=wv_t, wo_t=wo_t,
                  gate_t=gate_t, up_t=up_t, down_t=down_t,
                  gate_s2=gate_s2, up_s2=up_s2, down_s2=down_s2,
                  cosf=cosf, sinf=sinf, masks=masks, permM=permM,
                  identM=np.eye(128, dtype=np.float32))
    per_core = []
    for c in range(8):
        b, j = c // 4, c % 4
        own = np.r_[512 * j:512 * (j + 1), 512 * (7 - j):512 * (8 - j)]
        xb = x[b]
        d = dict(shared)
        d["x"] = xb
        d["xfm"] = np.ascontiguousarray(xb.T)
        d["xfm_own"] = np.ascontiguousarray(xb.T[:, own])
        d["x_own"] = np.ascontiguousarray(xb[own])
        d["cos_own"] = np.ascontiguousarray(cosf[:, own])
        d["sin_own"] = np.ascontiguousarray(sinf[:, own])
        per_core.append(d)
    return per_core


def build():
    _check_map()
    nc = bass.Bass("TRN2", target_bir_lowering=False)
    t_, v_, c_, g_, y_ = nc.tensor, nc.vector, nc.scalar, nc.gpsimd, nc.sync

    I = {}
    def din(name, shape, dtype=F32):
        I[name] = nc.dram_tensor(name, shape, dtype, kind="ExternalInput").ap()
    din("x", [T, HID]); din("xfm", [HID, T]); din("xfm_own", [HID, 1024])
    din("x_own", [1024, HID]); din("cos_own", [128, 1024]); din("sin_own", [128, 1024])
    din("wq_t", [HID, HID]); din("wk_t", [HID, 512]); din("wv_t", [HID, 512])
    din("wo_t", [HID, HID])
    din("gate_t", [HID, INTER]); din("up_t", [HID, INTER]); din("down_t", [INTER, HID])
    din("gate_s2", [128, 44]); din("up_s2", [128, 44]); din("down_s2", [128, 16])
    din("cosf", [128, T]); din("sinf", [128, T])
    din("masks", [128, 4 * 512], BF16); din("permM", [128, 128], BF16)
    din("identM", [128, 128])
    out = nc.dram_tensor("o", [1024, HID], F32, kind="ExternalOutput").ap()
    xn_dram = nc.dram_tensor("xn_dram", [HID, 1024], BF16, kind="Internal").ap()
    taps = {}
    A = {}
    for name, (off, shape, dt) in SB.items():
        A[name] = nc.alloc_sbuf_tensor_at(name, shape, dt, offset=off).ap()

    psA = nc.alloc_psum_tensor("psA", [128, 2048], F32).ap()
    psB = nc.alloc_psum_tensor("psB", [128, 2048], F32).ap()

    sem = {}
    C = {}
    for n in ["pe", "ac", "ve", "po", "s2", "ss", "sx", "sx1", "scn", "sf", "sf1",
              "sq2", "sr", "sr1", "sw", "sxo", "sod", "sod1", "sout",
              "gk", "gq", "gq1", "ggu", "ggu1", "gdn", "gdn1", "gwo"]:
        sem[n] = nc.semaphore(n).__enter__()
        C[n] = 0

    def inc(instr, s, n=1):
        C[s] += n
        instr.then_inc(sem[s], n)
        return C[s]

    def W(eng, s, val=None):
        if val is None:
            val = C[s]
        if val > 0:
            eng.wait_ge(sem[s], val)

    def dma(q, s, out, in_):
        W(q, s)
        return inc(q.dma_start(out=out, in_=in_), s, 16)

    cnt = {}
    body = nc.body("main", valid_engines=list(mybir.ALL_ENGINES))
    body.__enter__()

    v_.memset(A["c_m8"], -8.0)
    v_.memset(A["c_eps"], 1e-5)
    v_.memset(A["ones64"], 1.0)
    v_.memset(A["ones128"], 1.0)
    vview = A["V_sb"].rearrange("p (s k d) -> p s k d", s=32, k=8)
    v_.memset(A["kbfA"], 0.0)
    v_.memset(A["kbfQ"], 0.0)
    ve_init = inc(v_.memset(vview[:, :, :, 64:65], 1.0), "ve")

    for nm, src in [("identM", "identM"), ("permM", "permM"),
                    ("gateS", "gate_s2"), ("upS", "up_s2"), ("downS", "down_s2")]:
        dma(y_, "ss", A[nm], I[src])
    sd_small = C["ss"]

    wkvv = A["wkv"].rearrange("p (h w) -> p h w", h=16)
    dma(g_, "gk", wkvv[:, :, 0:512], I["wk_t"].rearrange("(h p) o -> p h o", p=128))
    gw_wkv = dma(g_, "gk", wkvv[:, :, 512:1024],
                 I["wv_t"].rearrange("(h p) o -> p h o", p=128))

    def ln_stats(xtile, ti, rows_g, rows_n):
        sc = A["scol"][:, 8 * (ti % 2): 8 * (ti % 2) + 8]
        W(c_, "sx")
        W(c_, "sx1")
        W(c_, "ve")
        c_.activation(A["sq_j"], xtile, AF.Square, accum_out=sc[:, 1:2])
        ac_sq = inc(c_.activation(A["sq_j2"], xtile, AF.Identity, accum_out=sc[:, 0:1]),
                    "ac")
        W(v_, "ac", ac_sq)
        vv = inc(v_.tensor_scalar_mul(sc[:, 2:3], sc[:, 0:1], 1.0 / 2048.0), "ve")
        W(v_, "ve", vv)
        vv = inc(v_.tensor_mul(sc[:, 3:4], sc[:, 2:3], sc[:, 2:3]), "ve")
        W(v_, "ve", vv)
        ve_var = inc(v_.scalar_tensor_tensor(sc[:, 4:5], sc[:, 1:2], 1.0 / 2048.0,
                                             sc[:, 3:4], ALU.mult, ALU.subtract), "ve")
        W(c_, "ve", ve_var)
        vv = inc(c_.activation(sc[:, 5:6], sc[:, 4:5], AF.Ln, bias=A["c_eps"]), "ac")
        W(c_, "ac", vv)
        ac_rstd = inc(c_.activation(sc[:, 6:7], sc[:, 5:6], AF.Exp, scale=-0.5), "ac")
        W(v_, "ac", ac_rstd)
        ve_nmr = inc(v_.scalar_tensor_tensor(sc[:, 7:8], sc[:, 2:3], -1.0, sc[:, 6:7],
                                             ALU.mult, ALU.mult), "ve")
        W(t_, "ve", ve_nmr)
        W(t_, "ss", sd_small)
        t_.transpose(psA[0:1, 0:128], sc[:, 6:7], A["identM"])
        pe_tr = inc(t_.transpose(psA[0:1, 512:640], sc[:, 7:8], A["identM"]), "pe")
        W(v_, "pe", pe_tr)
        v_.tensor_copy(rows_g[0:1, ti * 128:(ti + 1) * 128], psA[0:1, 0:128])
        ve_row = inc(v_.tensor_copy(rows_n[0:1, ti * 128:(ti + 1) * 128],
                                    psA[0:1, 512:640]), "ve")
        return ve_row, ac_sq

    for ti in range(8):
        buf = A["x_t"][:, (ti % 2) * 2048:(ti % 2) * 2048 + 2048]
        if ti >= 2:
            W(y_, "ve", cnt[("row", "own", ti - 2)])
            W(y_, "ac", cnt[("sq", "own", ti - 2)])
        dma(y_, "sx" if ti % 2 == 0 else "sx1", buf, I["x_own"][ti * 128:(ti + 1) * 128, :])
        r, s = ln_stats(buf, ti, A["rstd_own"], A["nmr_own"])
        cnt[("row", "own", ti)] = r
        cnt[("sq", "own", ti)] = s
    ve_rows_own = C["ve"]

    for tg in range(8):
        ve_rows = 0
        for ti in range(4):
            buf = A["x_t"][:, (ti % 2) * 2048:(ti % 2) * 2048 + 2048]
            prev = cnt.get(("row", tg, ti - 2), cnt.get(("row", tg - 1, ti + 2),
                           cnt.get(("row", "own", ti + 6), 0)))
            prevs = cnt.get(("sq", tg, ti - 2), cnt.get(("sq", tg - 1, ti + 2),
                            cnt.get(("sq", "own", ti + 6), 0)))
            W(y_, "ve", prev)
            W(y_, "ac", prevs)
            dma(y_, "sx" if ti % 2 == 0 else "sx1", buf,
                I["x"][tg * 512 + ti * 128: tg * 512 + (ti + 1) * 128, :])
            r, s = ln_stats(buf, ti, A["rows_g"], A["rows_n"])
            cnt[("row", tg, ti)] = r
            cnt[("sq", tg, ti)] = s
            ve_rows = r
        W(y_, "ve", cnt.get(("kadd", tg - 1, 3), 0))
        dma(y_, "scn", A["csn"][:, 0:512], I["cosf"][:, tg * 512:(tg + 1) * 512])
        dma(y_, "scn", A["csn"][:, 512:1024], I["sinf"][:, tg * 512:(tg + 1) * 512])
        sd_csn = C["scn"]
        W(t_, "ve", ve_rows)
        t_.matmul(psA[:, 1024:1536], A["ones128"], A["rows_g"], start=True, stop=True)
        pe_rep = inc(t_.matmul(psA[:, 1536:2048], A["ones128"], A["rows_n"],
                               start=True, stop=True), "pe")
        W(v_, "pe", pe_rep)
        W(v_, "po", cnt.get(("app", tg - 1, NHC - 1), 0))
        v_.tensor_copy(A["reps"][:, 0:512], psA[:, 1024:1536])
        ve_rep = inc(v_.tensor_copy(A["reps"][:, 512:1024], psA[:, 1536:2048]), "ve")
        xnr = A["xn_rot"] if tg % 2 == 0 else A["xn_rot2"]
        xfmv = I["xfm"].rearrange("(c p) t -> p c t", p=128)
        for hp in range(8):
            fb = A["xfm_t"][:, (hp % 2) * 1024:(hp % 2) * 1024 + 1024]
            prev = cnt.get(("appm", tg, 2 * hp - 3), cnt.get(("appm", tg - 1, 12 + 2 * hp + 1), 0))
            W(y_, "ve", prev)
            dma(y_, "sf" if hp % 2 == 0 else "sf1",
                fb.rearrange("p (c t) -> p c t", c=2),
                xfmv[:, 2 * hp:2 * hp + 2, tg * 512:(tg + 1) * 512])
            W(v_, "sf" if hp % 2 == 0 else "sf1")
            for sub in range(2):
                h = 2 * hp + sub
                if h == 0:
                    W(v_, "ve", ve_rep)
                ab = A["appt"][:, (h % 2) * 512:(h % 2) * 512 + 512]
                if h >= 2:
                    W(v_, "po", cnt.get(("app", tg, h - 2), 0))
                cnt[("appm", tg, h)] = inc(
                    v_.tensor_mul(ab, fb[:, sub * 512:(sub + 1) * 512],
                                  A["reps"][:, 0:512]), "ve")
                W(g_, "ve", cnt[("appm", tg, h)])
                if h == 0:
                    W(g_, "pe", cnt.get(("proj", tg - 2), 0))
                cnt[("app", tg, h)] = inc(
                    g_.tensor_add(xnr[:, h * 512:(h + 1) * 512], ab,
                                  A["reps"][:, 512:1024]), "po")
        W(t_, "gk", gw_wkv)
        W(t_, "po", cnt[("app", tg, NHC - 1)])
        W(t_, "ve", cnt.get(("kt1", tg - 1, 3), 0))
        for h in range(NHC):
            for ck in range(4):
                i = t_.matmul(psB[:, ck * 512:(ck + 1) * 512],
                              A["wkv"][:, h * 1024 + ck * 128: h * 1024 + (ck + 1) * 128],
                              xnr[:, h * 512:(h + 1) * 512],
                              start=(h == 0), stop=(h == NHC - 1))
        pe_K = inc(i, "pe")
        for ck in range(4):
            kb = A["kbfA"][:, (ck % 2) * 512:(ck % 2) * 512 + 512]
            W(v_, "pe", pe_K)
            W(v_, "pe", cnt.get(("ksw", tg, ck - 2), 0))
            ve_kb = inc(v_.tensor_copy(kb, psB[:, ck * 512:(ck + 1) * 512]), "ve")
            W(t_, "ve", ve_kb)
            W(t_, "ve", cnt.get(("kadd", tg, ck - 2), 0))
            cnt[("ksw", tg, ck)] = inc(
                t_.matmul(psA[:, (ck % 2) * 512:(ck % 2) * 512 + 512], A["permM"], kb,
                          start=True, stop=True), "pe")
            W(v_, "scn", sd_csn)
            W(v_, "pe", cnt[("ksw", tg, ck)])
            tb = A["t1A"][:, (ck % 2) * 512:(ck % 2) * 512 + 512]
            cnt[("kt1", tg, ck)] = inc(
                v_.tensor_mul(tb, psB[:, ck * 512:(ck + 1) * 512], A["csn"][:, 0:512]), "ve")
            vv = inc(v_.tensor_mul(psA[:, (2 + ck % 2) * 512:(2 + ck % 2) * 512 + 512],
                                   psA[:, (ck % 2) * 512:(ck % 2) * 512 + 512],
                                   A["csn"][:, 512:1024]), "ve")
            W(v_, "ve", vv)
            cnt[("kadd", tg, ck)] = inc(
                v_.tensor_add(A["K_sb"][:, ck * 4096 + tg * 512: ck * 4096 + (tg + 1) * 512],
                              tb, psA[:, (2 + ck % 2) * 512:(2 + ck % 2) * 512 + 512]), "ve")
        W(t_, "ve", cnt[("kadd", tg, 3)])
        W(t_, "ve", cnt.get(("vev", tg - 1, 3), 0))
        for h in range(NHC):
            for tv in range(4):
                i = t_.matmul(psA[:, tv * 512:(tv + 1) * 512],
                              xnr[:, h * 512 + tv * 128: h * 512 + (tv + 1) * 128],
                              A["wkv"][:, h * 1024 + 512: h * 1024 + 1024],
                              start=(h == 0), stop=(h == NHC - 1))
        pe_V = inc(i, "pe")
        cnt[("proj", tg)] = C["pe"]
        W(v_, "pe", pe_V)
        if tg == 0:
            W(v_, "ve", ve_init)
        for tv in range(4):
            dst = vview[:, tg * 4 + tv, :, 0:64]
            cnt[("vev", tg, tv)] = inc(
                v_.tensor_copy(dst, psA[:, tv * 512:(tv + 1) * 512]
                               .rearrange("p (k d) -> p k d", k=8)), "ve")

    ve_A_end = C["ve"]

    W(t_, "ve", ve_A_end)
    for half in range(2):
        t_.matmul(psA[:, half * 512:(half + 1) * 512], A["ones128"],
                  A["rstd_own"][0:1, half * 512:(half + 1) * 512], start=True, stop=True)
        i = t_.matmul(psA[:, 1024 + half * 512: 1024 + (half + 1) * 512], A["ones128"],
                      A["nmr_own"][0:1, half * 512:(half + 1) * 512], start=True, stop=True)
    pe_orep = inc(i, "pe")
    W(v_, "pe", pe_orep)
    v_.tensor_copy(A["reps_o"][:, 0:1024], psA[:, 0:1024])
    ve_orep = inc(v_.tensor_copy(A["reps_o"][:, 1024:2048], psA[:, 1024:2048]), "ve")

    for h in range(NHC):
        fb = A["xfm_ow"][:, (h % 2) * 1024:(h % 2) * 1024 + 1024]
        if h < 2:
            W(y_, "ve", cnt[("row", 7, 3)])
            W(y_, "ac", cnt[("sq", 7, 3)])
        W(y_, "ve", cnt.get(("oxn", h - 2), 0))
        dma(y_, "sf" if h % 2 == 0 else "sf1", fb, I["xfm_own"][h * 128:(h + 1) * 128, :])
        W(v_, "sf" if h % 2 == 0 else "sf1")
        if h == 0:
            W(v_, "ve", ve_orep)
        W(v_, "sw", cnt.get(("oxnd", h - 1), 0))
        vv = inc(v_.tensor_mul(A["t1Q"], fb, A["reps_o"][:, 0:1024]), "ve")
        W(v_, "ve", vv)
        cnt[("oxn", h)] = inc(v_.tensor_add(A["xnb_Q"][:, 0:1024], A["t1Q"],
                                            A["reps_o"][:, 1024:2048]), "ve")
        W(y_, "ve", cnt[("oxn", h)])
        dma(y_, "sw", xn_dram[h * 128:(h + 1) * 128, :], A["xnb_Q"][:, 0:1024])
        cnt[("oxnd", h)] = C["sw"]
    ou_xn = C["sw"]

    W(y_, "ve", cnt[("kadd", 7, 3)])
    dma(y_, "sq2", A["cosownQ"], I["cos_own"])
    dma(y_, "sq2", A["sinownQ"], I["sin_own"])
    sd_csown = C["sq2"]
    W(y_, "ve", cnt[("oxn", NHC - 1)])
    dma(y_, "sq2", A["masks"], I["masks"])
    sd_masks = C["sq2"]

    wqv = A["wq_st"].rearrange("p (b h w) -> p b h w", b=2, h=16)
    for op8 in range(8):
        if op8 < 2:
            W(g_, "pe", cnt[("proj", 7)])
        W(g_, "pe", cnt.get(("qmm", op8 - 2, NHC - 1), 0))
        gw_wq = dma(g_, "gq" if op8 % 2 == 0 else "gq1", wqv[:, op8 % 2],
                    I["wq_t"].rearrange("(h p) o -> p h o", p=128)[:, :, op8 * 256:(op8 + 1) * 256])
        xndv = xn_dram.rearrange("(c p) t -> p c t", p=128)
        for hp in range(8):
            xbp = A["xnp_Q"][:, (hp % 2) * 2048:(hp % 2) * 2048 + 2048]
            prev = cnt.get(("qmm", op8, 2 * hp - 3),
                           cnt.get(("qmm", op8 - 1, 13 + 2 * hp), 0))
            W(y_, "pe", prev)
            W(y_, "sw", 16 * (2 * hp + 2))
            dma(y_, "sr" if hp % 2 == 0 else "sr1",
                xbp.rearrange("p (c t) -> p c t", c=2), xndv[:, 2 * hp:2 * hp + 2, :])
            W(t_, "sr" if hp % 2 == 0 else "sr1")
            for sub in range(2):
                h = 2 * hp + sub
                xb = xbp[:, sub * 1024:(sub + 1) * 1024]
                if h == 0:
                    W(t_, "gq" if op8 % 2 == 0 else "gq1", gw_wq)
                    W(t_, "ve", cnt.get(("qrope", op8 - 1), ve_orep))
                for oc in range(2):
                    for th in range(2):
                        i = t_.matmul(psA[:, (oc * 2 + th) * 512:(oc * 2 + th + 1) * 512],
                                      wqv[:, op8 % 2, h, oc * 128:(oc + 1) * 128],
                                      xb[:, th * 512:(th + 1) * 512],
                                      start=(h == 0), stop=(h == NHC - 1))
                cnt[("qmm", op8, h)] = inc(i, "pe")
        pe_q = cnt[("qmm", op8, NHC - 1)]
        for oc in range(2):
            ch = op8 * 2 + oc
            W(v_, "pe", pe_q)
            W(v_, "pe", cnt.get(("qsw", ch - 1), 0))
            ve_kb = inc(v_.tensor_copy(A["kbfQ"], psA[:, oc * 1024:(oc + 1) * 1024]), "ve")
            W(t_, "ve", ve_kb)
            if ch == 0:
                W(t_, "ve", ve_A_end)
            else:
                W(t_, "ve", cnt.get(("qadd", ch - 1), 0))
            t_.matmul(psB[:, 0:512], A["permM"], A["kbfQ"][:, 0:512], start=True, stop=True)
            cnt[("qsw", ch)] = inc(
                t_.matmul(psB[:, 512:1024], A["permM"], A["kbfQ"][:, 512:1024],
                          start=True, stop=True), "pe")
            W(v_, "sq2", sd_csown)
            W(v_, "pe", cnt[("qsw", ch)])
            vv = inc(v_.tensor_mul(A["t1Q"], psA[:, oc * 1024:(oc + 1) * 1024],
                                   A["cosownQ"]), "ve")
            vv = inc(v_.tensor_mul(psB[:, 1024:2048], psB[:, 0:1024], A["sinownQ"]), "ve")
            W(v_, "ve", vv)
            cnt[("qadd", ch)] = inc(
                v_.tensor_add(A["Q_sb"][:, ch * 1024:(ch + 1) * 1024], A["t1Q"],
                              psB[:, 1024:2048]), "ve")
        cnt[("qrope", op8)] = C["ve"]

    ve_Q_end = C["ve"]
    pe_Q_end = C["pe"]
    ac_Q_end = C["ac"]
    pid_t = t_.compute_val(t_.partition_id() % 4)
    pid_c = c_.compute_val(c_.partition_id() % 4)
    pid_v = v_.compute_val(v_.partition_id() % 4)
    body.__exit__(None, None, None)

    snap = dict(C)
    final_counts = None
    for var in nc.Switch(engines=[t_, c_, v_], index=[pid_t, pid_c, pid_v], n=4):
        C.clear(); C.update(snap)
        j = var
        nodd = [0]
        cnt_s2 = [0]
        avfree_next = [0]
        pend = [None]
        firstL = [True]
        vmask_first = True
        exp_cnt = {}
        for gi, g in enumerate([j, 7 - j]):
            nsb = 4 * (g + 1)
            for ch in range(16):
                m = ch // 4
                av_free = avfree_next[0] if avfree_next[0] else C["ve"]
                pe_av = 0
                qk_cnt = {}

                def emit_qk(sb):
                    par = sb % 2
                    if (ch, sb - 2) in exp_cnt:
                        W(t_, "ac", exp_cnt[(ch, sb - 2)])
                    else:
                        W(t_, "ac")
                    for hh in range(2):
                        b0 = 64 * hh
                        i = t_.matmul(
                            psA[:, par * 1024 + hh * 512: par * 1024 + (hh + 1) * 512],
                            A["K_sb"][b0:b0 + 64,
                                      m * 4096 + sb * 128: m * 4096 + (sb + 1) * 128],
                            A["Q_sb"][b0:b0 + 64,
                                      ch * 1024 + gi * 512: ch * 1024 + (gi + 1) * 512],
                            start=True, stop=True)
                    qk_cnt[sb] = inc(i, "pe")

                if firstL[0]:
                    W(t_, "ve", ve_Q_end)
                    W(t_, "ac", ac_Q_end)
                    firstL[0] = False
                emit_qk(0)
                for sb in range(nsb):
                    par = sb % 2
                    if sb == 1 and pend[0] is not None:
                        pend[0]()
                        pend[0] = None
                    if sb + 1 < nsb:
                        emit_qk(sb + 1)
                    W(c_, "pe", qk_cnt[sb])
                    eb = A["Ebuf"][:, (sb % 2) * 1024:(sb % 2) * 1024 + 1024]
                    ac_exp = inc(c_.activation(eb, psA[:, par * 1024:(par + 1) * 1024],
                                               AF.Exp, bias=A["c_m8"], scale=0.125), "ac")
                    exp_cnt[(ch, sb)] = ac_exp
                    diag = sb >= 4 * g
                    if diag:
                        k = sb - 4 * g
                        if vmask_first:
                            W(v_, "sq2", sd_masks)
                            vmask_first = False
                        W(v_, "ac", ac_exp)
                        W(v_, "pe")
                        for hh in range(2):
                            i = v_.tensor_mul(A["Emask"][:, hh * 512:(hh + 1) * 512],
                                              eb[:, hh * 512:(hh + 1) * 512],
                                              A["masks"][:, k * 512:(k + 1) * 512])
                        ve_mask = inc(i, "ve")
                        W(t_, "ve", ve_mask)
                        src = A["Emask"]
                    else:
                        W(t_, "ac", ac_exp)
                        src = eb
                    if sb == 0:
                        W(t_, "ve", av_free)
                        W(t_, "s2", cnt_s2[0])
                    for hh in range(2):
                        kv = 2 * m + hh
                        i = t_.matmul(
                            psB[0:65, hh * 512:(hh + 1) * 512],
                            A["V_sb"].rearrange("p (s w) -> p s w", s=32)
                            [:, sb, kv * 65:(kv + 1) * 65],
                            src[:, hh * 512:(hh + 1) * 512],
                            start=(sb == 0), stop=(sb == nsb - 1))
                    pe_av = inc(i, "pe")
                W(v_, "pe", pe_av)
                ve_nd = inc(v_.tensor_copy(A["nd_sb"], psB[0:65, 0:1024]), "ve")
                avfree_next[0] = ve_nd

                def mk_norm(gi, ch, ve_nd):
                    def norm():
                        for hh in range(2):
                            head = 8 * (ch // 4) + (ch % 4) + 4 * hh
                            W(v_, "ve", ve_nd)
                            W(v_, "pe")
                            ve_rd = inc(
                                v_.reciprocal(A["rdbuf"],
                                              A["nd_sb"][64:65,
                                                         hh * 512:(hh + 1) * 512]),
                                "ve")
                            W(t_, "ve", ve_rd)
                            i = t_.matmul(psB[0:64, 1024 + hh * 512:
                                              1024 + hh * 512 + 512],
                                          A["ones64"], A["rdbuf"],
                                          start=True, stop=True)
                            pe_bc = inc(i, "pe")
                            bcb = A["bcbuf0"] if hh == 0 else A["bcbuf1"]
                            W(v_, "pe", pe_bc)
                            ve_bc = inc(v_.tensor_copy(bcb,
                                                       psB[0:64, 1024 + hh * 512:
                                                           1024 + hh * 512 + 512]),
                                        "ve")
                            W(v_, "ve", ve_bc)
                            dst_cols = slice((head // 2) * 1024 + gi * 512,
                                             (head // 2) * 1024 + (gi + 1) * 512)
                            if head % 2 == 0:
                                i = v_.tensor_mul(A["A_sb"][0:64, dst_cols],
                                                  A["nd_sb"][0:64,
                                                             hh * 512:(hh + 1) * 512],
                                                  bcb)
                                inc(i, "ve")
                            else:
                                par2 = nodd[0] % 2
                                W(v_, "s2",
                                  cnt_s2[0] - 16 if nodd[0] >= 2 else 0)
                                i = v_.tensor_mul(
                                    A["antmp"][:, par2 * 512:(par2 + 1) * 512],
                                    A["nd_sb"][0:64, hh * 512:(hh + 1) * 512], bcb)
                                ve_an = inc(i, "ve")
                                W(c_, "ve", ve_an)
                                W(c_, "s2")
                                inc(c_.dma_start(
                                    out=A["A_sb"][64:128, dst_cols],
                                    in_=A["antmp"][:, par2 * 512:(par2 + 1) * 512]),
                                    "s2", 16)
                                cnt_s2[0] = C["s2"]
                                nodd[0] += 1
                    return norm
                pend[0] = mk_norm(gi, ch, ve_nd)
        pend[0]()
        pend[0] = None
        if final_counts is None:
            final_counts = dict(C)
        else:
            assert final_counts == C, "switch variant sem divergence"
    C.clear(); C.update(final_counts)

    body2 = nc.body("tail", valid_engines=list(mybir.ALL_ENGINES))
    body2.__enter__()
    ve_B_end = C["ve"]
    pe_B_end = C["pe"]
    ac_B_end = C["ac"]
    s2_B_end = C["s2"]

    for h in range(NHC):
        if h == 0:
            W(y_, "pe", pe_Q_end)
            W(y_, "ve", ve_Q_end)
        dma(y_, "sxo", A["xn_own"][:, h * 1024:(h + 1) * 1024],
            xn_dram[h * 128:(h + 1) * 128, :])
    sd_xno = C["sxo"]

    guv_g = A["gu_g"].rearrange("p (b h w) -> p b h w", b=2, h=16)
    guv_u = A["gu_u"].rearrange("p (b h w) -> p b h w", b=2, h=16)
    gatev = I["gate_t"].rearrange("(h p) i -> p h i", p=128)
    upv = I["up_t"].rearrange("(h p) i -> p h i", p=128)
    for ic in range(NIC):
        if ic < 2:
            W(g_, "ve", ve_B_end)
        W(g_, "pe", cnt.get(("fmm", ic - 2, NHC - 1), 0))
        gs_ = "ggu" if ic % 2 == 0 else "ggu1"
        dma(g_, gs_, guv_g[:, ic % 2], gatev[:, :, ic * 128:(ic + 1) * 128])
        gw_gu = dma(g_, gs_, guv_u[:, ic % 2], upv[:, :, ic * 128:(ic + 1) * 128])
        ps = psA if ic % 2 == 0 else psB
        for h in range(NHC):
            if h == 0:
                W(t_, "sxo", sd_xno)
                W(t_, "ggu" if ic % 2 == 0 else "ggu1", gw_gu)
                if ic < 2:
                    W(t_, "ve", ve_B_end)
                    W(t_, "ac", ac_B_end)
                else:
                    W(t_, "ac", cnt.get(("fsil", ic - 2), 0))
                    W(t_, "ve", cnt.get(("fH", ic - 2), 0))
            for th in range(2):
                t_.matmul(ps[:, th * 512:(th + 1) * 512],
                          guv_g[:, ic % 2, h, :],
                          A["xn_own"][:, h * 1024 + th * 512: h * 1024 + (th + 1) * 512],
                          start=(h == 0), stop=(h == NHC - 1))
            for th in range(2):
                i = t_.matmul(ps[:, 1024 + th * 512: 1024 + (th + 1) * 512],
                              guv_u[:, ic % 2, h, :],
                              A["xn_own"][:, h * 1024 + th * 512: h * 1024 + (th + 1) * 512],
                              start=(h == 0), stop=(h == NHC - 1))
        cnt[("fmm", ic, NHC - 1)] = inc(i, "pe")
        W(c_, "pe", cnt[("fmm", ic, NHC - 1)])
        W(c_, "ve", cnt.get(("fH", ic - 1), 0))
        cnt[("fsil", ic)] = inc(
            c_.activation(A["gtC"], ps[:, 0:1024], AF.Silu,
                          scale=A["gateS"][:, ic:ic + 1]), "ac")
        W(v_, "ac", cnt[("fsil", ic)])
        if ic == 0:
            W(v_, "s2", s2_B_end)
        cnt[("fH", ic)] = inc(
            v_.scalar_tensor_tensor(A["H_sb"][:, ic * 1024:(ic + 1) * 1024],
                                    ps[:, 1024:2048], A["upS"][:, ic:ic + 1], A["gtC"],
                                    ALU.mult, ALU.mult), "ve")
    ve_C_end = C["ve"]
    pe_C_end = C["pe"]

    dnv = A["dn_st"].rearrange("p (b i w) -> p b i w", b=2, i=22)
    wov = A["wo_st"].rearrange("p (b h w) -> p b h w", b=2, h=16)
    downv = I["down_t"].rearrange("(i p) o -> p i o", p=128)
    wodv = I["wo_t"].rearrange("(h p) o -> p h o", p=128)
    for ob in range(8):
        for half in range(2):
            if ob == 0:
                W(g_, "pe", pe_C_end)
            W(g_, "pe", cnt.get(("dmm", ob - 1, half), 0))
            cnt[("gwdn", ob, half)] = dma(
                g_, "gdn" if half == 0 else "gdn1", dnv[:, half],
                downv[:, half * 22:(half + 1) * 22, ob * 256:(ob + 1) * 256])
        if ob == 0:
            W(g_, "pe", pe_C_end)
        W(g_, "pe", cnt.get(("wmm", ob - 2), 0))
        cnt[("gwwo", ob)] = dma(g_, "gwo", wov[:, ob % 2],
                                wodv[:, :, ob * 256:(ob + 1) * 256])
        for i44 in range(NIC):
            half, idx = (0, i44) if i44 < 22 else (1, i44 - 22)
            if i44 == 0:
                W(t_, "ve", ve_C_end)
                W(t_, "gdn", cnt[("gwdn", ob, 0)])
                W(t_, "ac", cnt.get(("dg", ob - 1, 1), 0))
                W(t_, "ve", cnt.get(("dadd", ob - 1, 1, 7), 0))
            if i44 == 22:
                W(t_, "gdn1", cnt[("gwdn", ob, 1)])
            for oc in range(2):
                for th in range(2):
                    i = t_.matmul(psA[:, (oc * 2 + th) * 512:(oc * 2 + th + 1) * 512],
                                  dnv[:, half, idx, oc * 128:(oc + 1) * 128],
                                  A["H_sb"][:, i44 * 1024 + th * 512:
                                            i44 * 1024 + (th + 1) * 512],
                                  start=(i44 == 0), stop=(i44 == NIC - 1))
            if i44 == 21:
                cnt[("dmm", ob, 0)] = inc(i, "pe")
        cnt[("dmm", ob, 1)] = pe_dn = inc(i, "pe")
        for h in range(NHC):
            if h == 0:
                W(t_, "gwo", cnt[("gwwo", ob)])
                W(t_, "ve", ve_B_end)
                W(t_, "s2", s2_B_end)
                W(t_, "ve", cnt.get(("dsg", ob - 1, 1), 0))
            for oc in range(2):
                for th in range(2):
                    i = t_.matmul(psB[:, (oc * 2 + th) * 512:(oc * 2 + th + 1) * 512],
                                  wov[:, ob % 2, h, oc * 128:(oc + 1) * 128],
                                  A["A_sb"][:, h * 1024 + th * 512:
                                            h * 1024 + (th + 1) * 512],
                                  start=(h == 0), stop=(h == NHC - 1))
        cnt[("wmm", ob)] = pe_wo = inc(i, "pe")
        for oc in range(2):
            occ = 2 * ob + oc
            W(c_, "pe", pe_dn)
            W(c_, "ve", cnt.get(("dsg", ob, oc - 1), cnt.get(("dsg", ob - 1, 1), 0)))
            cnt[("dg", ob, oc)] = inc(
                c_.activation(A["g_D"], psA[:, oc * 1024:(oc + 1) * 1024], AF.Silu,
                              scale=A["downS"][:, occ:occ + 1]), "ac")
            W(v_, "pe", pe_wo)
            W(v_, "ac", cnt[("dg", ob, oc)])
            cnt[("dsg", ob, oc)] = inc(
                v_.tensor_mul(A["sg_D"][:, oc * 1024:(oc + 1) * 1024],
                              psB[:, oc * 1024:(oc + 1) * 1024], A["g_D"]), "ve")
        for tt in range(8):
            xo = A["xo_D"][:, (tt % 2) * 256:(tt % 2) * 256 + 256]
            if (ob, tt) == (0, 0):
                W(y_, "pe", pe_C_end)
            W(y_, "ve", cnt.get(("dadd", ob, 1, tt - 2),
                                cnt.get(("dadd", ob - 1, 1, 6 + tt), 0)))
            sods = "sod" if tt % 2 == 0 else "sod1"
            dma(y_, sods, xo, I["x_own"][tt * 128:(tt + 1) * 128,
                                         ob * 256:(ob + 1) * 256])
            sd_xo = C[sods]
            for oc in range(2):
                occ = 2 * ob + oc
                W(t_, "ve", cnt[("dsg", ob, 1)])
                W(t_, "ac", cnt[("dg", ob, 1)])
                W(t_, "ve", cnt.get(("dadd", ob, oc, tt - 2),
                                    cnt.get(("dadd", ob - 1, oc, 6 + tt), 0)))
                bank = 2 * (tt % 2) + oc
                i = t_.transpose(psA[:, bank * 512: bank * 512 + 128],
                                 A["sg_D"][:, oc * 1024 + tt * 128:
                                           oc * 1024 + (tt + 1) * 128], A["identM"])
                pe_tr = inc(i, "pe")
                W(v_, "pe", pe_tr)
                W(v_, sods, sd_xo)
                W(v_, "sout", cnt.get(("dout", ob, tt - 4),
                                      cnt.get(("dout", ob - 1, 4 + tt), 0)))
                cnt[("dadd", ob, oc, tt)] = inc(
                    v_.tensor_add(A["out_D"][:, (tt % 4) * 256 + oc * 128:
                                             (tt % 4) * 256 + oc * 128 + 128],
                                  psA[:, bank * 512: bank * 512 + 128],
                                  xo[:, oc * 128:(oc + 1) * 128]), "ve")
            W(y_, "ve", cnt[("dadd", ob, 1, tt)])
            dma(y_, "sout", out[tt * 128:(tt + 1) * 128, ob * 256:(ob + 1) * 256],
                A["out_D"][:, (tt % 4) * 256:(tt % 4) * 256 + 256])
            cnt[("dout", ob, tt)] = C["sout"]

    W(y_, "sout", C["sout"])
    body2.__exit__(None, None, None)
    return nc


def kernel(**inputs):
    per_core = host_layout(inputs)
    nc = build()
    res = run_bass_kernel_spmd(nc, per_core, core_ids=list(range(8)))
    outp = np.zeros((2, 4096, 2048), np.float32)
    for c in range(8):
        b, j = c // 4, c % 4
        r = np.asarray(res.results[c]["o"], np.float32)
        outp[b, 512 * j:512 * (j + 1)] = r[0:512]
        outp[b, 512 * (7 - j):512 * (8 - j)] = r[512:1024]
    return outp


if __name__ == "__main__":
    pass
